# version 05
# baseline (speedup 1.0000x reference)
import numpy as np
import ml_dtypes
from contextlib import ExitStack

import concourse.bass as bass
import concourse.mybir as mybir
from concourse.bass_utils import run_bass_kernel_spmd

F32 = mybir.dt.float32
BF16 = mybir.dt.bfloat16
AF = mybir.ActivationFunctionType
ALU = mybir.AluOpType
AX = mybir.AxisListType

NCORE = 8
B, S, D = 2, 8192, 1024
CT = 2048
NTILE = CT // 128
DIN = 2048
NH, HD, DST, DCONV = 32, 64, 64, 4
CONVD = DIN + 2 * DST
DPROJ = 2 * DIN + 2 * DST + NH
MLP_INNER = 4096
LCH = 128
NCH = CT // LCH
EPS = 1e-5
BF = ml_dtypes.bfloat16
DBG_N = 1
DBG_DH = 1
DBG_DQ = 4


class Sem:
    __slots__ = ("h", "n")

    def __init__(self, h):
        self.h = h
        self.n = 0


class Prog:
    ENGINES = {"t": "tensor", "v": "vector", "s": "scalar", "g": "gpsimd", "y": "sync"}

    def __init__(self):
        self.nc = bass.Bass("TRN2", target_bir_lowering=False)
        self.ctx = ExitStack()
        self.ops = {e: [] for e in self.ENGINES}

    def inp(self, name, shape, dtype=F32):
        return self.nc.dram_tensor(name, shape, dtype, kind="ExternalInput").ap()

    def outp(self, name, shape, dtype=F32):
        return self.nc.dram_tensor(name, shape, dtype, kind="ExternalOutput").ap()

    def sbuf(self, name, shape, dtype):
        return self.ctx.enter_context(self.nc.sbuf_tensor(name, shape, dtype))

    def psum(self, name, shape, dtype=F32):
        return self.ctx.enter_context(self.nc.psum_tensor(name, shape, dtype))

    def sem(self, name):
        return Sem(self.ctx.enter_context(self.nc.semaphore(name)))

    def op(self, e, f):
        self.ops[e].append(f)

    def dma(self, e, out, in_, sem):
        sem.n += 16
        self.op(e, lambda E, o=out, i=in_, s=sem.h: E.dma_start(out=o, in_=i).then_inc(s, 16))

    def wait(self, e, sem, val=None):
        v = sem.n if val is None else val
        self.op(e, lambda E, s=sem.h, v=v: E.wait_ge(s, v))

    def step(self, e, f, sig=None):
        if sig is not None:
            sig.n += 1
            self.op(e, lambda E, f=f, s=sig.h: f(E).then_inc(s, 1))
        else:
            self.op(e, f)

    def mmacc(self, out_ap, pairs, sig):
        n = len(pairs)
        for j, (l, r) in enumerate(pairs):
            if j < n - 1:
                self.op("t", lambda E, o=out_ap, l=l, r=r, st=(j == 0):
                        E.matmul(o, l, r, start=st, stop=False))
            else:
                self.step("t", lambda E, o=out_ap, l=l, r=r, st=(j == 0):
                          E.matmul(o, l, r, start=st, stop=True), sig)

    def settle(self, *sems):
        for e in ("t", "v", "s", "g"):
            for sm in sems:
                self.wait(e, sm)

    def barrier(self, *engs):
        if not engs:
            engs = tuple(self.ENGINES)
        if not hasattr(self, "_bar"):
            self._bar = {e: self.sem(f"bar_{e}") for e in self.ENGINES}
        engs = [e for e in engs]
        for e in engs:
            sm = self._bar[e]
            self.step(e, lambda E: E.nop(), sm)
        targets = {e: self._bar[e].n for e in engs}
        for e in engs:
            for e2 in engs:
                if e2 != e:
                    self.wait(e, self._bar[e2], targets[e2])

    def finish(self):
        block = self.ctx.enter_context(self.nc.Block())
        for e, name in self.ENGINES.items():
            if self.ops[e]:
                def runner(E, ops=tuple(self.ops[e])):
                    for f in ops:
                        f(E)
                getattr(block, name)(runner)
        self.ctx.close()
        return self.nc



def host_constants():
    c = {}
    c["ident_b"] = np.eye(128, dtype=BF)
    c["ident_f"] = np.eye(128, dtype=np.float32)
    c["cum_incl"] = np.triu(np.ones((128, 128), np.float32))
    c["cum_rev"] = np.tril(np.ones((128, 128), np.float32), -1)
    c["cum_excl"] = np.triu(np.ones((128, 128), np.float32), 1)
    c["ones_f"] = np.ones((128, 128), np.float32)
    c["ones_b"] = np.ones((128, 128), BF)
    er = np.zeros((32, 2048), np.float32)
    for h in range(32):
        er[h, 64 * h:64 * h + 64] = 1.0
    c["erep32"] = er
    sb = np.zeros((32, 32 * 128), np.float32)
    for h in range(32):
        sb[h, 128 * h:128 * h + 128] = 1.0
    c["selb32"] = sb
    c["mask_le"] = np.triu(np.ones((128, 128), np.float32)).copy()
    hs = np.zeros((128, 2), np.float32)
    hs[:64, 0] = 1.0
    hs[64:, 1] = 1.0
    c["hsel2"] = hs
    b2 = np.zeros((2, 256), np.float32)
    b2[0, :128] = 1.0
    b2[1, 128:] = 1.0
    c["bc2"] = b2
    cm = np.zeros((128, 4 * 512), BF)
    for o in range(4):
        for kk in range(128):
            cm[kk, o * 512 + max(0, kk + o * 128):(o + 1) * 512] = 1.0
    c["cmask4"] = cm
    sw = np.zeros((128, 64), BF)
    for blk in (0, 64):
        for m in range(64):
            sw[blk + (m + 32) % 64, m] = 1.0
    c["swap64"] = sw
    sg = np.ones((128, 1), np.float32)
    sg[np.arange(128) % 64 < 32] = -1.0
    c["signcol"] = sg
    return c


_HC = host_constants()
CONST_SPECS = {k: (v.shape, v.dtype) for k, v in _HC.items()}


def use_consts(p, names):
    ins, tiles = {}, {}
    for n in names:
        shape, dt = CONST_SPECS[n]
        mdt = BF16 if dt == BF else F32
        ins[n] = p.inp("c_" + n, list(shape), mdt)
        tiles[n] = p.sbuf("k_" + n, list(shape), mdt)

    def load(ld):
        for n in names:
            pp = CONST_SPECS[n][0][0]
            p.dma("y", tiles[n][0:pp, :], ins[n][:, :], ld)
    return tiles, load


def const_inputs(names):
    return {"c_" + n: np.ascontiguousarray(_HC[n]) for n in names}



def emit_ln(p, xin, xout, scr, stats, sc, ev, n_feat=1024, epscol=None, extra_outs=()):
    inv_n = 1.0 / n_feat
    pd = xin.shape[0]
    sumsq, s_, mu, var, rstd = (stats[0:pd, i:i + 1] for i in range(5))
    p.step("s", lambda E, a=xin, o=scr[0:pd, 0:n_feat], acc=sumsq: E.activation(
        o, a, AF.Square, accum_out=acc), sc)
    p.step("v", lambda E, a=xin, o=s_: E.reduce_sum(o, a, axis=AX.X), ev)
    p.wait("v", sc)
    p.wait("v", ev)
    p.step("v", lambda E, i_=s_, o=mu, k=inv_n: E.tensor_scalar_mul(o, i_, k), ev)
    p.step("v", lambda E, i_=sumsq, o=var, k=inv_n: E.tensor_scalar_mul(o, i_, k), ev)
    tmp = stats[0:pd, 5:6]
    p.wait("v", ev)
    p.step("v", lambda E, a=mu, o=tmp: E.tensor_mul(o, a, a), ev)
    p.wait("v", ev)
    p.step("v", lambda E, a=var, b=tmp, o=var: E.tensor_sub(o, a, b), ev)
    p.wait("s", ev)
    p.step("s", lambda E, i_=var, o=rstd, e=epscol[0:pd, 0:1]: E.activation(o, i_, AF.Sqrt, bias=e), sc)
    p.wait("v", sc)
    p.step("v", lambda E, i_=rstd, o=rstd: E.reciprocal(o, i_), ev)
    p.wait("v", ev)
    p.step("v", lambda E, a=xin, o=xout, m=mu, r=rstd: E.tensor_scalar(
        o, a, m, r, ALU.subtract, ALU.mult), ev)
    for o2 in extra_outs:
        p.step("v", lambda E, a=xin, o=o2, m=mu, r=rstd: E.tensor_scalar(
            o, a, m, r, ALU.subtract, ALU.mult), ev)


def emit_tr(p, src_ap, dst_ap, pst, ident, mm, ev, ncols=128):
    p.step("t", lambda E, i_=src_ap, o=pst, idn=ident: E.transpose(o, i_, idn), mm)
    p.wait("v", mm)
    p.step("v", lambda E, o=dst_ap, i_=pst: E.tensor_copy(o, i_), ev)
    p.wait("t", ev)



def build_A1():
    p = Prog()
    x = p.inp("x", [CT, D])
    xh = p.inp("xh", [3, D])
    w_in = p.inp("w_in", [D, DPROJ])
    conv_w = p.inp("conv_w", [CONVD, DCONV])
    conv_b = p.inp("conv_b", [CONVD, 1])
    dt_bias = p.inp("dt_bias", [NH, 1])
    a_log = p.inp("a_log", [NH, 1])
    d_param = p.inp("d_param", [NH, 1])
    consts, load_consts = use_consts(
        p, ["ident_b", "ident_f", "cum_incl", "cum_rev", "ones_f", "erep32", "selb32", "mask_le"])

    o_y1 = p.outp("y1", [CT, DIN], BF16)
    o_z = p.outp("z", [CT, DIN], BF16)
    o_cfm = p.outp("cfm", [DST, CT], BF16)
    o_logda = p.outp("logda", [NH, CT])
    o_lndt = p.outp("lndt", [NH, CT])
    o_hloc = p.outp("hloc", [DST, NH * HD])
    o_lsum = p.outp("lsum", [NH, 1])

    XSW = CT + 3

    xt = p.sbuf("xt", [128, 2 * D], F32)
    xh_s = p.sbuf("xh_s", [3, D], F32)
    xn_t = p.sbuf("xn_t", [128, 2 * D], BF16)
    xnh = p.sbuf("xnh", [3, D], BF16)
    scratch8 = p.sbuf("scratch8", [128, 2048], F32)
    scr = scratch8
    stats = p.sbuf("stats", [128, 8], F32)
    fcon = p.sbuf("fcon", [128, 2], F32)
    xnf = p.sbuf("xnf", [128, 8 * CT], BF16)
    xnhf = p.sbuf("xnhf", [128, 24], BF16)
    wbuf = p.sbuf("wbuf", [128, 8 * 512], BF16)
    wsm = p.sbuf("wsm", [128, 8 * 160], BF16)
    zst = p.sbuf("zst", [128, 2 * 512], BF16)
    xsf = p.sbuf("xsf", [128, 2 * XSW], BF16)
    xs_ch = p.sbuf("xs_ch", [128, 16 * 128], BF16)
    bpre = p.sbuf("bpre", [64, XSW], BF16)
    cpre = p.sbuf("cpre", [64, XSW], BF16)
    bfm = p.sbuf("bfm", [64, CT], BF16)
    cfm = p.sbuf("cfm_s", [64, CT], BF16)
    dtraw = p.sbuf("dtraw", [NH, CT], F32)
    dtf = p.sbuf("dtf", [NH, CT], F32)
    lndt = p.sbuf("lndt_s", [NH, CT], F32)
    logda = p.sbuf("logda_s", [NH, CT], F32)
    dtcols = p.sbuf("dtcols", [NH, 4], F32)
    dbc = p.sbuf("dbc", [128, NH], F32)
    cwdiag = p.sbuf("cwdiag", [128, 16 * 4 * 128], BF16)
    cwdbc = p.sbuf("cwdbc", [64, 8 * 64], BF16)
    convbt = p.sbuf("convbt", [128, 18], F32)
    convwt = p.sbuf("convwt", [128, 72], F32)
    lend = p.sbuf("lend", [NH, NCH + 2], F32)
    xs_tm = p.sbuf("xs_tm", [128, DIN], BF16)
    dtx2 = p.sbuf("dtx2", [128, DIN], BF16)
    b_tm = p.sbuf("b_tm", [128, DST], BF16)
    ldc_tm = p.sbuf("ldc_tm", [128, NH], F32)
    lndt_tm = p.sbuf("lndt_tm", [128, NH], F32)
    dtf_tm = p.sbuf("dtf_tm", [128, NH], F32)
    lc_tm = p.sbuf("lc_tm", [128, NH], F32)
    lc_fm = p.sbuf("lc_fm", [NH, 128], F32)
    rev_e = p.sbuf("rev_e", [128, NH], F32)
    drcol = p.sbuf("drcol", [128, NH], F32)
    e2_tm = p.sbuf("e2_tm", [128, NH], F32)
    gm = p.sbuf("gm", [128, 128], F32)
    expd = p.sbuf("expd", [128, 128], F32)
    wmat = p.sbuf("wmat", [128, 128], BF16)
    y_a = p.sbuf("y_a", [128, DIN], BF16)
    y_tile = p.sbuf("y_tile", [128, DIN], BF16)
    htm = p.sbuf("htm", [64, DIN], F32)
    htm_b = p.sbuf("htm_b", [64, DIN], BF16)
    hupd = scratch8
    gam64 = p.sbuf("gam64", [NH, HD], F32)
    elend = p.sbuf("elend", [NH, 1], F32)

    ps1 = p.psum("ps1", [128, 512])
    ps2 = p.psum("ps2", [128, 512])
    ps3 = p.psum("ps3", [128, 512])
    psy1 = p.psum("psy1", [128, 1024])
    psy2 = p.psum("psy2", [128, 1024])
    pst_b = p.psum("pst_b", [128, 1024], BF16)
    pst_f = ps3[:, 0:128]
    psg = ps1[:, 0:128]

    ld = p.sem("ld")
    wv = p.sem("wv")
    mm = p.sem("mm")
    ev = p.sem("ev")
    sc = p.sem("sc")
    st = p.sem("st")

    idb = consts["ident_b"][:, :]
    idf = consts["ident_f"][:, :]

    load_consts(ld)
    p.dma("y", xh_s[:, :], xh[:, :], ld)
    p.dma("y", dtcols[0:NH, 0:1], dt_bias[:, :], ld)
    p.dma("y", dtcols[0:NH, 1:2], a_log[:, :], ld)
    p.dma("y", dtcols[0:NH, 3:4], d_param[:, :], ld)
    for i in range(16):
        p.dma("y", convbt[:, i:i + 1], conv_b[i * 128:(i + 1) * 128, :], ld)
        p.dma("y", convwt[:, 4 * i:4 * i + 4], conv_w[i * 128:(i + 1) * 128, :], ld)
    p.dma("y", convbt[0:64, 16:17], conv_b[DIN:DIN + 64, :], ld)
    p.dma("y", convbt[0:64, 17:18], conv_b[DIN + 64:DIN + 128, :], ld)
    p.dma("y", convwt[0:64, 64:68], conv_w[DIN:DIN + 64, :], ld)
    p.dma("y", convwt[0:64, 68:72], conv_w[DIN + 64:DIN + 128, :], ld)
    p.op("g", lambda E: E.memset(fcon[:, 0:1], EPS))
    p.op("g", lambda E: E.memset(fcon[:, 1:2], 1.0))
    for k in range(8):
        p.dma("g", wsm[:, k * 160:(k + 1) * 160], w_in[k * 128:(k + 1) * 128, 2 * DIN:DPROJ], wv)
    p.barrier("y", "g", "s", "v", "t")
    p.settle(ld, wv)

    for i in range(NTILE):
        buf = i % 2
        p.dma("y", xt[:, buf * D:(buf + 1) * D], x[i * 128:(i + 1) * 128, :], ld)
        p.wait("s", ld)
        p.wait("v", ld)
        emit_ln(p, xt[:, buf * D:(buf + 1) * D], xn_t[:, buf * D:(buf + 1) * D],
                scr, stats, sc, ev, epscol=fcon)
        p.wait("t", ev)
        for k in range(8):
            emit_tr(p, xn_t[:, buf * D + k * 128:buf * D + (k + 1) * 128],
                    xnf[:, k * CT + i * 128:k * CT + (i + 1) * 128],
                    pst_b[:, 0:128], idb, mm, ev)
        p.wait("y", ev)
    p.wait("s", ld)
    p.wait("v", ld)
    emit_ln(p, xh_s[0:3, :], xnh[0:3, :], scr, stats, sc, ev, n_feat=1024, epscol=fcon)
    p.wait("t", ev)
    for k in range(8):
        emit_tr(p, xnh[0:3, k * 128:(k + 1) * 128], xnhf[:, 3 * k:3 * k + 3],
                pst_b[0:128, 0:3], idb[0:3, 0:3], mm, ev)
    p.barrier("t", "v", "s", "y", "g")

    for n in range(4):
        if n > 0:
            p.wait("g", mm)
        for k in range(8):
            p.dma("g", wbuf[:, k * 512:(k + 1) * 512],
                  w_in[k * 128:(k + 1) * 128, n * 512:(n + 1) * 512], wv)
        p.wait("t", wv)
        for i in range(NTILE):
            pb = (ps1, ps2)[i % 2]
            zs = zst[:, (i % 2) * 512:(i % 2 + 1) * 512]
            p.mmacc(pb[:, :],
                    [(xnf[:, k * CT + i * 128:k * CT + (i + 1) * 128],
                      wbuf[:, k * 512:(k + 1) * 512]) for k in range(8)], mm)
            p.wait("s", mm)
            p.step("s", lambda E, o=zs, i_=pb[:, :]: E.copy(o, i_), sc)
            p.wait("y", sc)
            p.dma("y", o_z[i * 128:(i + 1) * 128, n * 512:(n + 1) * 512], zs, st)
            p.wait("t", sc, sc.n - 1)
            p.wait("s", st, st.n - 16)
    p.barrier("t", "s", "g", "y")

    for mi in range(16):
        for d_ in range(DCONV):
            p.step("v", lambda E, o=cwdiag[:, (mi * 4 + d_) * 128:(mi * 4 + d_ + 1) * 128],
                   i_=idb, s_=convwt[:, 4 * mi + d_:4 * mi + d_ + 1]:
                   E.tensor_scalar_mul(o, i_, s_), ev)
    for bi, base in ((0, 64), (1, 68)):
        for d_ in range(DCONV):
            p.step("v", lambda E, o=cwdbc[0:64, (bi * 4 + d_) * 64:(bi * 4 + d_ + 1) * 64],
                   i_=idb[0:64, 0:64], s_=convwt[0:64, base + d_:base + d_ + 1]:
                   E.tensor_scalar_mul(o, i_, s_), ev)
    p.barrier("v", "t", "s")

    xs_scr = p.nc.dram_tensor("xs_scr", [16, 128, CT], BF16).ap()
    for m in range(16):
        buf = m % 2
        xtile = xsf[:, buf * XSW:(buf + 1) * XSW]
        for k in range(8):
            p.dma("g", wbuf[:, k * 128:(k + 1) * 128],
                  w_in[k * 128:(k + 1) * 128, DIN + m * 128:DIN + (m + 1) * 128], wv)
        p.wait("t", wv)
        for n in range(4):
            pb = (ps1, ps2)[n % 2]
            p.mmacc(pb[:, :],
                    [(wbuf[:, k * 128:(k + 1) * 128],
                      xnf[:, k * CT + n * 512:k * CT + (n + 1) * 512]) for k in range(8)], mm)
            p.wait("s", mm)
            p.step("s", lambda E, o=xtile[:, 3 + n * 512:3 + (n + 1) * 512],
                   i_=pb[:, :]: E.copy(o, i_), sc)
            p.wait("t", sc, sc.n - 1)
        p.mmacc(ps3[0:128, 0:3],
                [(wbuf[:, k * 128:(k + 1) * 128], xnhf[:, 3 * k:3 * k + 3]) for k in range(8)], mm)
        p.wait("s", mm)
        p.step("s", lambda E, o=xtile[:, 0:3], i_=ps3[0:128, 0:3]: E.copy(o, i_), sc)
        p.wait("t", sc)
        p.wait("g", sc, sc.n)
        p.barrier("t", "s")
        for n in range(4):
            pb = (ps1, ps2)[n % 2]
            p.mmacc(pb[:, :],
                    [(cwdiag[:, (m * 4 + d_) * 128:(m * 4 + d_ + 1) * 128],
                      xtile[:, n * 512 + d_:n * 512 + d_ + 512]) for d_ in range(DCONV)], mm)
            p.wait("s", mm)
            p.step("s", lambda E, o=xtile[:, n * 512:(n + 1) * 512],
                   i_=pb[:, :], b=convbt[:, m:m + 1]:
                   E.activation(o, i_, AF.Silu, bias=b), sc)
            p.wait("t", sc, sc.n - 1)
        p.wait("y", sc, sc.n)
        p.dma("y", xs_scr[m, :, :], xtile[:, 0:CT], st)
        p.wait("s", st, st.n - 16)
    for (dst, c0) in ((bpre, 0), (cpre, 64)):
        for n in range(4):
            pb = (ps1, ps2)[n % 2]
            p.mmacc(pb[0:64, :],
                    [(wsm[:, k * 160 + c0:k * 160 + c0 + 64],
                      xnf[:, k * CT + n * 512:k * CT + (n + 1) * 512]) for k in range(8)], mm)
            p.wait("s", mm)
            p.step("s", lambda E, o=dst[0:64, 3 + n * 512:3 + (n + 1) * 512],
                   i_=pb[0:64, :]: E.copy(o, i_), sc)
            p.wait("t", sc, sc.n - 1)
        p.mmacc(ps3[0:64, 0:3],
                [(wsm[:, k * 160 + c0:k * 160 + c0 + 64], xnhf[:, 3 * k:3 * k + 3])
                 for k in range(8)], mm)
        p.wait("s", mm)
        p.step("s", lambda E, o=dst[0:64, 0:3], i_=ps3[0:64, 0:3]: E.copy(o, i_), sc)
        p.wait("t", sc)
    for n in range(4):
        pb = (ps1, ps2)[n % 2]
        p.mmacc(pb[0:NH, :],
                [(wsm[:, k * 160 + 128:k * 160 + 160],
                  xnf[:, k * CT + n * 512:k * CT + (n + 1) * 512]) for k in range(8)], mm)
        p.wait("s", mm)
        p.step("s", lambda E, o=dtraw[0:NH, n * 512:(n + 1) * 512], i_=pb[0:NH, :]: E.copy(o, i_), sc)
        p.wait("t", sc, sc.n - 1)
    p.barrier("t", "s", "v", "g", "y")

    for (src_, dst, bi, bcol) in ((bpre, bfm, 0, 16), (cpre, cfm, 1, 17)):
        for n in range(4):
            pb = (ps1, ps2)[n % 2]
            p.mmacc(pb[0:64, :],
                    [(cwdbc[0:64, (bi * 4 + d_) * 64:(bi * 4 + d_ + 1) * 64],
                      src_[0:64, n * 512 + d_:n * 512 + d_ + 512]) for d_ in range(DCONV)], mm)
            p.wait("s", mm)
            p.step("s", lambda E, o=dst[0:64, n * 512:(n + 1) * 512],
                   i_=pb[0:64, :], b=convbt[0:64, bcol:bcol + 1]:
                   E.activation(o, i_, AF.Silu, bias=b), sc)
            p.wait("t", sc, sc.n - 1)
    p.barrier("t", "s", "v", "y")
    p.dma("y", o_cfm[:, :], cfm[0:64, :], st)

    p.step("s", lambda E, o=dtcols[0:NH, 2:3], i_=dtcols[0:NH, 1:2]: E.activation(o, i_, AF.Exp), sc)
    p.wait("v", sc)
    p.step("v", lambda E, o=dtraw[0:NH, :], i_=dtraw[0:NH, :], s_=dtcols[0:NH, 0:1]:
           E.tensor_scalar_add(o, i_, s_), ev)
    p.wait("s", ev)
    p.step("s", lambda E, o=logda[0:NH, :], i_=dtraw[0:NH, :]: E.activation(o, i_, AF.Abs), sc)
    p.step("s", lambda E, o=logda[0:NH, :], i_=logda[0:NH, :]: E.activation(o, i_, AF.Exp, scale=-1.0), sc)
    p.step("s", lambda E, o=logda[0:NH, :], i_=logda[0:NH, :], b=fcon[0:NH, 1:2]:
           E.activation(o, i_, AF.Ln, bias=b), sc)
    p.wait("v", sc)
    p.step("v", lambda E, o=dtf[0:NH, :], i_=dtraw[0:NH, :]:
           E.tensor_single_scalar(o, i_, 0.0, ALU.max), ev)
    p.step("v", lambda E, o=dtf[0:NH, :], a=dtf[0:NH, :], b=logda[0:NH, :]: E.tensor_add(o, a, b), ev)
    p.wait("s", ev)
    p.step("s", lambda E, o=lndt[0:NH, :], i_=dtf[0:NH, :]: E.activation(o, i_, AF.Ln), sc)
    p.wait("v", sc)
    p.step("v", lambda E, o=logda[0:NH, :], i_=dtf[0:NH, :], s_=dtcols[0:NH, 2:3]:
           E.tensor_scalar(o, i_, s_, -1.0, ALU.mult, ALU.mult), ev)
    p.step("v", lambda E, o=lend[0:NH, NCH:NCH + 1], i_=logda[0:NH, :]:
           E.reduce_sum(o, i_, axis=AX.X), ev)
    for cc_ in range(NCH):
        p.step("v", lambda E, o=lend[0:NH, cc_:cc_ + 1], i_=logda[0:NH, cc_ * LCH:(cc_ + 1) * LCH]:
               E.reduce_sum(o, i_, axis=AX.X), ev)
    p.step("v", lambda E, o=htm[0:64, :]: E.memset(o, 0.0), ev)
    p.step("v", lambda E, o=htm_b[0:64, :]: E.memset(o, 0.0), ev)
    p.step("v", lambda E, o=hupd[0:NH, 0:NH], i_=consts["ident_f"][0:NH, 0:NH],
           s_=dtcols[0:NH, 3:4]: E.tensor_scalar_mul(o, i_, s_), ev)
    p.wait("t", ev)
    p.mmacc(pst_f[0:128, 0:NH], [(consts["ones_f"][0:NH, :], hupd[0:NH, 0:NH])], mm)
    p.wait("v", mm)
    p.step("v", lambda E, o=dbc[0:128, 0:NH], i_=pst_f[0:128, 0:NH]: E.tensor_copy(o, i_), ev)
    p.wait("t", ev)
    p.barrier("v", "s", "t", "y")
    p.dma("y", o_logda[:, :], logda[0:NH, :], st)
    p.dma("y", o_lndt[:, :], lndt[0:NH, :], st)
    p.dma("y", o_lsum[:, :], lend[0:NH, NCH:NCH + 1], st)

    selb = consts["selb32"]
    maskle = consts["mask_le"]
    cumi = consts["cum_incl"]
    cumr = consts["cum_rev"]
    erep = consts["erep32"]
    onesf = consts["ones_f"]

    for cc_ in range(NCH):
        t0 = cc_ * LCH
        for mtile in range(16):
            p.dma("y", xs_ch[:, mtile * 128:(mtile + 1) * 128], xs_scr[mtile, :, t0:t0 + 128], ld)
        for (src_, dst) in ((logda, ldc_tm), (lndt, lndt_tm), (dtf, dtf_tm)):
            p.step("t", lambda E, i_=src_[0:NH, t0:t0 + 128], o=pst_f[0:128, 0:NH],
                   idn=idf[0:NH, 0:NH]: E.transpose(o, i_, idn), mm)
            p.wait("v", mm)
            p.step("v", lambda E, o=dst[0:128, 0:NH], i_=pst_f[0:128, 0:NH]: E.tensor_copy(o, i_), ev)
            p.wait("t", ev)
        p.step("t", lambda E, i_=bfm[0:64, t0:t0 + 128], o=pst_b[0:128, 0:64],
               idn=idb[0:64, 0:64]: E.transpose(o, i_, idn), mm)
        p.wait("v", mm)
        p.step("v", lambda E, o=b_tm[0:128, 0:64], i_=pst_b[0:128, 0:64]: E.tensor_copy(o, i_), ev)
        p.wait("t", ev)
        p.wait("t", ld)
        for h in range(NH):
            mtile, sub = h // 2, (h % 2) * 64
            p.step("t", lambda E, i_=xs_ch[sub:sub + 64, mtile * 128:(mtile + 1) * 128],
                   o=pst_b[0:128, 0:64], idn=idb[sub:sub + 64, sub:sub + 64]:
                   E.transpose(o, i_, idn), mm)
            p.wait("v", mm)
            p.step("v", lambda E, o=xs_tm[:, h * 64:(h + 1) * 64], i_=pst_b[0:128, 0:64]:
                   E.tensor_copy(o, i_), ev)
            p.wait("t", ev)
        p.mmacc(pst_f[0:128, 0:NH], [(cumi[:, :], ldc_tm[0:128, 0:NH])], mm)
        p.wait("v", mm)
        p.step("v", lambda E, o=lc_tm[0:128, 0:NH], i_=pst_f[0:128, 0:NH]: E.tensor_copy(o, i_), ev)
        p.wait("t", ev)
        p.mmacc(pst_f[0:128, 0:NH], [(cumr[:, :], ldc_tm[0:128, 0:NH])], mm)
        p.wait("s", mm)
        p.step("s", lambda E, o=rev_e[0:128, 0:NH], i_=pst_f[0:128, 0:NH]: E.activation(o, i_, AF.Exp), sc)
        p.wait("t", sc)
        p.step("t", lambda E, i_=lc_tm[0:128, 0:NH], o=pst_f[0:NH, 0:128], idn=idf:
               E.transpose(o, i_, idn), mm)
        p.wait("v", mm)
        p.step("v", lambda E, o=lc_fm[0:NH, :], i_=pst_f[0:NH, 0:128]: E.tensor_copy(o, i_), ev)
        p.wait("t", ev)
        p.wait("s", ev)
        p.step("s", lambda E, o=e2_tm[0:128, 0:NH], i_=lc_tm[0:128, 0:NH]: E.activation(o, i_, AF.Exp), sc)
        p.wait("v", sc)
        p.step("v", lambda E, o=drcol[0:128, 0:NH], a=dtf_tm[0:128, 0:NH], b=rev_e[0:128, 0:NH]:
               E.tensor_mul(o, a, b), ev)
        p.wait("v", ev)
        for h in range(NH):
            p.step("v", lambda E, o=dtx2[:, h * 64:(h + 1) * 64], i_=xs_tm[:, h * 64:(h + 1) * 64],
                   s_=drcol[0:128, h:h + 1]: E.tensor_scalar_mul(o, i_, s_), ev)
        p.mmacc(psg, [(bfm[0:64, t0:t0 + 128], cfm[0:64, t0:t0 + 128])], mm)
        p.wait("v", mm)
        p.step("v", lambda E, o=gm[:, :], a=psg, b=maskle[:, :]: E.tensor_mul(o, a, b), ev)
        p.wait("t", ev)

        p.wait("v", st)

        for wave in range(2):
            hh0 = wave * 16
            for hi in range(16):
                h = hh0 + hi
                p.mmacc(pst_f[:, :], [(selb[0:NH, h * 128:(h + 1) * 128], lc_fm[0:NH, :])], mm)
                p.wait("v", mm)
                p.step("v", lambda E, o=expd[:, :], i_=pst_f[:, :], s1=lc_tm[0:128, h:h + 1]:
                       E.tensor_scalar(o, i_, s1, 0.0, ALU.subtract, ALU.min), ev)
                p.wait("t", ev)
                p.wait("s", ev)
                p.step("s", lambda E, o=expd[:, :], i_=expd[:, :], b=lndt_tm[0:128, h:h + 1]:
                       E.activation(o, i_, AF.Exp, bias=b), sc)
                p.wait("v", sc)
                p.step("v", lambda E, o=wmat[:, :], a=gm[:, :], b=expd[:, :]: E.tensor_mul(o, a, b), ev)
                p.wait("t", ev)
                p.mmacc(psy1[:, hi * 64:(hi + 1) * 64],
                        [(wmat[:, :], xs_tm[:, h * 64:(h + 1) * 64])], mm)
                p.mmacc(psy2[:, hi * 64:(hi + 1) * 64],
                        [(cfm[0:64, t0:t0 + 128], htm_b[0:64, h * 64:(h + 1) * 64])], mm)
            p.wait("v", mm)
            for hi in range(16):
                h = hh0 + hi
                p.step("v", lambda E, o=y_a[:, h * 64:(h + 1) * 64],
                       i0=xs_tm[:, h * 64:(h + 1) * 64], s_=dbc[0:128, h:h + 1],
                       i1=psy1[:, hi * 64:(hi + 1) * 64]:
                       E.scalar_tensor_tensor(o, i0, s_, i1, ALU.mult, ALU.add), ev)
            for hi in range(16):
                h = hh0 + hi
                p.step("v", lambda E, o=y_tile[:, h * 64:(h + 1) * 64],
                       i0=psy2[:, hi * 64:(hi + 1) * 64], s_=e2_tm[0:128, h:h + 1],
                       i1=y_a[:, h * 64:(h + 1) * 64]:
                       E.scalar_tensor_tensor(o, i0, s_, i1, ALU.mult, ALU.add), ev)
            p.wait("t", ev)
        p.wait("y", ev)
        p.dma("y", o_y1[t0:t0 + 128, :], y_tile[:, :], st)

        for h in range(NH):
            pdst = psy1 if h < 16 else psy2
            p.mmacc(pdst[0:64, (h % 16) * 64:(h % 16 + 1) * 64],
                    [(b_tm[0:128, 0:64], dtx2[:, h * 64:(h + 1) * 64])], mm)
        p.wait("s", ev)
        p.step("s", lambda E, o=elend[0:NH, 0:1], i_=lend[0:NH, cc_:cc_ + 1]: E.activation(o, i_, AF.Exp), sc)
        p.wait("v", sc)
        p.step("v", lambda E, o=gam64[0:NH, :], i_=onesf[0:NH, 0:HD], s_=elend[0:NH, 0:1]:
               E.tensor_scalar_mul(o, i_, s_), ev)
        p.wait("t", ev)
        for n in range(4):
            pb = (ps1, ps2)[n % 2]
            p.mmacc(pb[0:64, :], [(gam64[0:NH, :], erep[0:NH, n * 512:(n + 1) * 512])], mm)
            p.wait("v", mm)
            p.step("v", lambda E, o=hupd[0:64, n * 512:(n + 1) * 512],
                   a=pb[0:64, :], b=htm[0:64, n * 512:(n + 1) * 512]:
                   E.tensor_mul(o, a, b), ev)
            p.wait("t", ev)
        p.wait("v", mm)
        p.step("v", lambda E, o=htm[0:64, 0:1024], a=hupd[0:64, 0:1024], b=psy1[0:64, :]:
               E.tensor_add(o, a, b), ev)
        p.step("v", lambda E, o=htm[0:64, 1024:2048], a=hupd[0:64, 1024:2048], b=psy2[0:64, :]:
               E.tensor_add(o, a, b), ev)
        p.step("v", lambda E, o=htm_b[0:64, :], i_=htm[0:64, :]: E.tensor_copy(o, i_), ev)
        p.wait("t", ev)
        p.barrier("t", "v", "s", "y")

    p.dma("y", o_hloc[:, :], htm[0:64, :], st)
    p.wait("y", st)
    return p.finish()



def _split_tokens(a):
    out = []
    for b in range(B):
        for j in range(4):
            out.append(np.ascontiguousarray(a[b, j * CT:(j + 1) * CT]))
    return out


def run_A1(inputs):
    x = np.asarray(inputs["L0_attn_hidden_tokens"])
    consts = const_inputs(["ident_b", "ident_f", "cum_incl", "cum_rev", "ones_f",
                           "erep32", "selb32", "mask_le"])
    shards = _split_tokens(x)
    in_maps = []
    for c in range(NCORE):
        b, j = c // 4, c % 4
        if j == 0:
            xh = np.zeros((3, D), np.float32)
        else:
            xh = np.ascontiguousarray(x[b, j * CT - 3:j * CT])
        m = {
            "x": shards[c], "xh": xh,
            "w_in": np.asarray(inputs["L0_attn_W_in"]),
            "conv_w": np.asarray(inputs["L0_attn_conv_w"]),
            "conv_b": np.asarray(inputs["L0_attn_conv_b"]).reshape(-1, 1),
            "dt_bias": np.asarray(inputs["L0_attn_dt_bias"]).reshape(-1, 1),
            "a_log": np.asarray(inputs["L0_attn_A_log"]).reshape(-1, 1),
            "d_param": np.asarray(inputs["L0_attn_D_param"]).reshape(-1, 1),
        }
        m.update(consts)
        in_maps.append(m)
    nc = build_A1()
    res = run_bass_kernel_spmd(nc, in_maps, core_ids=list(range(NCORE)))
    return res.results


def kernel(**inputs):
    r1 = run_A1(inputs)
    r2 = run_A2(inputs, r1)
    r3 = run_A3(inputs, r2)
    r4 = run_A4(inputs, r3)
    r5 = run_A5(inputs, r3, r4)
    r6 = run_A6(inputs, r5)
    r7 = run_A7(inputs, r5, r6)
    out = np.zeros((B, S, D), np.float32)
    for c in range(NCORE):
        b, j = c // 4, c % 4
        out[b, j * CT:(j + 1) * CT] = np.asarray(r7[c]["x4"])
    return out



def emit_rms(p, xin, xout, scr, stats, sc, ev, n_feat, epscol, extra_outs=()):
    pd = xin.shape[0]
    sumsq = stats[0:pd, 0:1]
    rstd = stats[0:pd, 1:2]
    p.step("s", lambda E, a=xin, o=scr[0:pd, 0:n_feat], acc=sumsq: E.activation(
        o, a, AF.Square, accum_out=acc), sc)
    p.wait("s", sc)
    p.step("s", lambda E, i_=sumsq, o=rstd, e=epscol[0:pd, 0:1], k=1.0 / n_feat:
           E.activation(o, i_, AF.Sqrt, bias=e, scale=k), sc)
    p.wait("v", sc)
    p.step("v", lambda E, i_=rstd, o=rstd: E.reciprocal(o, i_), ev)
    p.wait("v", ev)
    p.step("v", lambda E, a=xin, o=xout, r=rstd: E.tensor_scalar_mul(o, a, r), ev)
    for o2 in extra_outs:
        p.step("v", lambda E, a=xin, o=o2, r=rstd: E.tensor_scalar_mul(o, a, r), ev)


def build_A2(upto=99):
    p = Prog()
    x = p.inp("x", [CT, D])
    y1 = p.inp("y1", [CT, DIN], BF16)
    z_in = p.inp("z", [CT, DIN], BF16)
    cfm_in = p.inp("cfm", [DST, CT], BF16)
    logda_in = p.inp("logda", [NH, CT])
    hloc_all = p.inp("hloc_all", [NCORE, DST, NH * HD])
    lsum_all = p.inp("lsum_all", [NCORE, NH])
    cmask = p.inp("cmask", [NCORE, NCORE])
    cvalid = p.inp("cvalid", [NCORE, 1])
    w_out = p.inp("w_out", [DIN, D])
    norm_w = p.inp("norm_w", [DIN, 1])
    rms_w = p.inp("rms_w", [D, 1])
    mlp_w1 = p.inp("mlp_w1", [D, MLP_INNER])
    mlp_b1 = p.inp("mlp_b1", [128, MLP_INNER // 128])
    mlp_w2 = p.inp("mlp_w2", [MLP_INNER, D])
    mlp_b2 = p.inp("mlp_b2", [128, D // 128])
    gate_wt = p.inp("gate_wt", [D, 8])
    consts, load_consts = use_consts(p, ["ident_b", "ident_f", "cum_incl", "cum_excl",
                                         "ones_f", "erep32"])

    o_x1 = p.outp("x1", [CT, D])
    o_xn = p.outp("xn", [CT, D])
    o_m1 = p.outp("m1", [CT, 8])
    o_m2 = p.outp("m2", [CT, 8])
    o_probs = p.outp("probs", [CT, 8])
    o_c1 = p.outp("c1", [CT, 8])
    o_c2 = p.outp("c2", [CT, 8])
    o_tot = p.outp("tot", [1, 8])

    x1_scr = p.nc.dram_tensor("x1_scr", [CT, D], F32).ap()
    xp_scr = p.nc.dram_tensor("xp_scr", [CT, D], F32).ap()
    in_scr = p.nc.dram_tensor("in_scr", [32, 128, CT], BF16).ap()

    cfm = p.sbuf("cfm_s", [64, CT], BF16)
    logda = p.sbuf("logda_s", [NH, CT], F32)
    lend = p.sbuf("lend", [NH, NCH + 2], F32)
    lpre = p.sbuf("lpre", [NH, NCH], F32)
    lpre_T = p.sbuf("lpre_T", [NCH, NH], F32)
    lprerow = p.sbuf("lprerow", [1, NH], F32)
    hin = p.sbuf("hin", [64, DIN], F32)
    hin_b = p.sbuf("hin_b", [64, DIN], BF16)
    hstr = p.sbuf("hstr", [64, DIN], F32)
    coefr = p.sbuf("coefr", [8, NH], F32)
    coefT = p.sbuf("coefT", [NH, 8], F32)
    crep = p.sbuf("crep", [8, DIN], F32)
    crow = p.sbuf("crow", [1, DIN], F32)
    lsums = p.sbuf("lsums", [NCORE, NH], F32)
    cmsk = p.sbuf("cmsk", [NCORE, NCORE], F32)
    cval = p.sbuf("cval", [NCORE, 1], F32)
    wouts = p.sbuf("wouts", [128, 16 * D], BF16)
    nwcol = p.sbuf("nwcol", [128, 16], F32)
    rwcol = p.sbuf("rwcol", [128, 8], F32)
    xstr = p.sbuf("xstr", [128, 2 * D], F32)
    y1str = p.sbuf("y1str", [128, 2 * DIN], BF16)
    zstr = p.sbuf("zstr", [128, 2 * DIN], BF16)
    ldc_tm = p.sbuf("ldc_tm", [128, NH], F32)
    e2p = p.sbuf("e2p", [128, NH], F32)
    y_t = p.sbuf("y_t", [128, DIN], BF16)
    zsil = p.sbuf("zsil", [128, DIN], BF16)
    u_t = p.sbuf("u_t", [128, DIN], BF16)
    un_t = p.sbuf("un_t", [128, DIN], BF16)
    u_fm = p.sbuf("u_fm", [128, DIN], BF16)
    scratch8 = p.sbuf("scratch8", [128, 2048], F32)
    stats = p.sbuf("stats", [128, 8], F32)
    fcon = p.sbuf("fcon", [128, 2], F32)
    x1st = p.sbuf("x1st", [128, 2 * D], F32)
    hnf = wouts
    hnt = p.sbuf("hnt", [128, D], BF16)
    wstr = p.sbuf("wstr", [128, 8 * 128], BF16)
    w2str = p.sbuf("w2str", [128, 2 * D], BF16)
    b1c = p.sbuf("b1c", [128, 32], F32)
    b2c = p.sbuf("b2c", [128, 8], F32)
    instr_s = p.sbuf("instr_s", [128, 2 * 512], BF16)
    mo_fm = p.sbuf("mo_fm", [128, 8 * 512], BF16)
    mo_tm = p.sbuf("mo_tm", [128, D], F32)
    gwt = p.sbuf("gwt", [128, 8 * 8], BF16)
    xn_t = p.sbuf("xn_t", [128, D], F32)
    xn_b = p.sbuf("xn_b", [128, D], BF16)
    xnfr = p.sbuf("xnfr", [128, 8 * 128], BF16)
    rlog = p.sbuf("rlog", [128, 8], F32)
    rprob = p.sbuf("rprob", [128, 8], F32)
    rstat = p.sbuf("rstat", [128, 6], F32)
    rm1 = p.sbuf("rm1", [128, 8], F32)
    rm2 = p.sbuf("rm2", [128, 8], F32)
    rmsum = p.sbuf("rmsum", [128, 8], F32)
    rc1 = p.sbuf("rc1", [128, 8], F32)
    rc2 = p.sbuf("rc2", [128, 8], F32)
    offr = p.sbuf("offr", [1, 8], F32)

    ps1 = p.psum("ps1", [128, 512])
    ps2 = p.psum("ps2", [128, 512])
    ps3 = p.psum("ps3", [128, 512])
    psy1 = p.psum("psy1", [128, 1024])
    psy2 = p.psum("psy2", [128, 1024])
    pst_b = p.psum("pst_b", [128, 1024], BF16)
    pso = psy1
    pst_f = ps3[:, 0:128]

    ld = p.sem("ld")
    wv = p.sem("wv")
    mm = p.sem("mm")
    ev = p.sem("ev")
    sc = p.sem("sc")
    st = p.sem("st")

    idb = consts["ident_b"][:, :]
    idf = consts["ident_f"][:, :]
    cumi = consts["cum_incl"]
    cume = consts["cum_excl"]
    onesf = consts["ones_f"]

    load_consts(ld)
    p.dma("y", cfm[0:64, :], cfm_in[:, :], ld)
    p.dma("y", logda[0:NH, :], logda_in[:, :], ld)
    p.dma("y", lsums[0:NCORE, :], lsum_all[:, :], ld)
    p.dma("y", cmsk[0:NCORE, :], cmask[:, :], ld)
    p.dma("y", cval[0:NCORE, :], cvalid[:, :], ld)
    for k in range(16):
        p.dma("y", nwcol[:, k:k + 1], norm_w[k * 128:(k + 1) * 128, :], ld)
    for k in range(8):
        p.dma("y", rwcol[:, k:k + 1], rms_w[k * 128:(k + 1) * 128, :], ld)
    p.dma("y", b1c[:, :], mlp_b1[:, :], ld)
    p.dma("y", b2c[:, :], mlp_b2[:, :], ld)
    for k in range(8):
        p.dma("g", gwt[:, k * 8:(k + 1) * 8], gate_wt[k * 128:(k + 1) * 128, :], wv)
    for k in range(16):
        p.dma("g", wouts[:, k * D:(k + 1) * D], w_out[k * 128:(k + 1) * 128, :], wv)
    p.op("g", lambda E: E.memset(fcon[:, 0:1], EPS))
    p.op("g", lambda E: E.memset(fcon[:, 1:2], 1.0))
    p.op("v", lambda E: E.memset(offr[0:1, :], 0.0))
    p.barrier("y", "g", "s", "v", "t")
    p.settle(ld, wv)

    for k in range(16):
        p.step("v", lambda E, o=wouts[:, k * D:(k + 1) * D], i_=wouts[:, k * D:(k + 1) * D],
               s_=nwcol[:, k:k + 1]: E.tensor_scalar_mul(o, i_, s_), ev)

    for cc_ in range(NCH):
        p.step("v", lambda E, o=lend[0:NH, cc_:cc_ + 1], i_=logda[0:NH, cc_ * LCH:(cc_ + 1) * LCH]:
               E.reduce_sum(o, i_, axis=AX.X), ev)
    p.wait("v", ev)
    p.step("v", lambda E, o=lpre[0:NH, 0:1]: E.memset(o, 0.0), ev)
    for cc_ in range(1, NCH):
        p.wait("v", ev)
        p.step("v", lambda E, o=lpre[0:NH, cc_:cc_ + 1], a=lpre[0:NH, cc_ - 1:cc_],
               b=lend[0:NH, cc_ - 1:cc_]: E.tensor_add(o, a, b), ev)
    p.wait("t", ev)
    p.step("t", lambda E, i_=lpre[0:NH, 0:NCH], o=pst_f[0:NCH, 0:NH], idn=idf[0:NH, 0:NH]:
           E.transpose(o, i_, idn), mm)
    p.wait("v", mm)
    p.step("v", lambda E, o=lpre_T[0:NCH, :], i_=pst_f[0:NCH, 0:NH]: E.tensor_copy(o, i_), ev)
    p.wait("t", ev)

    p.mmacc(pst_f[0:NCORE, 0:NH], [(cmsk[0:NCORE, :], lsums[0:NCORE, :])], mm)
    p.wait("s", mm)
    p.step("s", lambda E, o=coefr[0:NCORE, :], i_=pst_f[0:NCORE, 0:NH]: E.activation(o, i_, AF.Exp), sc)
    p.wait("v", sc)
    p.step("v", lambda E, o=coefr[0:NCORE, :], i_=coefr[0:NCORE, :], s_=cval[0:NCORE, 0:1]:
           E.tensor_scalar_mul(o, i_, s_), ev)
    p.wait("t", ev)
    p.step("t", lambda E, i_=coefr[0:NCORE, 0:NH], o=pst_f[0:NH, 0:NCORE], idn=idf[0:NCORE, 0:NCORE]:
           E.transpose(o, i_, idn), mm)
    p.wait("v", mm)
    p.step("v", lambda E, o=coefT[0:NH, :], i_=pst_f[0:NH, 0:NCORE]: E.tensor_copy(o, i_), ev)
    p.wait("t", ev)
    for n in range(4):
        pb = (ps1, ps2)[n % 2]
        p.mmacc(pb[0:NCORE, :], [(coefT[0:NH, :], consts["erep32"][0:NH, n * 512:(n + 1) * 512])], mm)
        p.wait("v", mm)
        p.step("v", lambda E, o=crep[0:NCORE, n * 512:(n + 1) * 512], i_=pb[0:NCORE, :]:
               E.tensor_copy(o, i_), ev)
        p.wait("t", ev)
    p.step("v", lambda E, o=hin[0:64, :]: E.memset(o, 0.0), ev)
    p.wait("v", ev)
    tmp64 = scratch8[0:64, 0:512]
    for i in range(NCORE):
        p.wait("y", ev)
        p.dma("y", hstr[0:64, 0:DIN], hloc_all[i, :, :], ld)
        p.wait("t", ld)
        p.wait("v", ld)
        for n in range(4):
            p.mmacc(ps1[0:1, :], [(idf[0:NCORE, i:i + 1], crep[0:NCORE, n * 512:(n + 1) * 512])], mm)
            p.wait("v", mm)
            p.step("v", lambda E, o=crow[0:1, n * 512:(n + 1) * 512], i_=ps1[0:1, :]:
                   E.tensor_copy(o, i_), ev)
            p.wait("t", ev)
            p.mmacc(ps2[0:64, :], [(onesf[0:1, 0:64], crow[0:1, n * 512:(n + 1) * 512])], mm)
            p.wait("v", mm)
            p.step("v", lambda E, o=tmp64, a=ps2[0:64, :],
                   b=hstr[0:64, n * 512:(n + 1) * 512]:
                   E.tensor_mul(o, a, b), ev)
            p.wait("v", ev)
            p.step("v", lambda E, o=hin[0:64, n * 512:(n + 1) * 512],
                   a=hin[0:64, n * 512:(n + 1) * 512], b=tmp64: E.tensor_add(o, a, b), ev)
            p.wait("t", ev)
            p.wait("v", ev)
    p.step("v", lambda E, o=hin_b[0:64, :], i_=hin[0:64, :]: E.tensor_copy(o, i_), ev)
    p.barrier("y", "t", "v", "s", "g")
    if upto <= 1:
        p.wait("y", st)
        return p.finish()

    for cc_ in range(NCH):
        t0 = cc_ * LCH
        buf = cc_ % 2
        p.dma("y", y1str[:, buf * DIN:(buf + 1) * DIN], y1[t0:t0 + 128, :], ld)
        p.dma("y", zstr[:, buf * DIN:(buf + 1) * DIN], z_in[t0:t0 + 128, :], ld)
        p.dma("y", xstr[:, buf * D:(buf + 1) * D], x[t0:t0 + 128, :], ld)
        p.mmacc(ps1[0:1, 0:NH], [(idf[0:NCH, cc_:cc_ + 1], lpre_T[0:NCH, 0:NH])], mm)
        p.wait("v", mm)
        p.step("v", lambda E, o=lprerow[0:1, :], i_=ps1[0:1, 0:NH]: E.tensor_copy(o, i_), ev)
        p.step("t", lambda E, i_=logda[0:NH, t0:t0 + 128], o=pst_f[0:128, 0:NH],
               idn=idf[0:NH, 0:NH]: E.transpose(o, i_, idn), mm)
        p.wait("v", mm)
        p.step("v", lambda E, o=ldc_tm[0:128, 0:NH], i_=pst_f[0:128, 0:NH]: E.tensor_copy(o, i_), ev)
        p.wait("t", ev)
        p.mmacc(pst_f[0:128, 0:NH],
                [(cumi[:, :], ldc_tm[0:128, 0:NH]), (onesf[0:1, 0:128], lprerow[0:1, :])], mm)
        p.wait("s", mm)
        p.step("s", lambda E, o=e2p[0:128, 0:NH], i_=pst_f[0:128, 0:NH]: E.activation(o, i_, AF.Exp), sc)
        p.wait("t", sc)
        p.wait("v", ld)
        p.wait("v", st, st.n)
        for wave in range(2):
            hh0 = wave * 16
            for hi in range(16):
                h = hh0 + hi
                pdst = psy1 if wave == 0 else psy2
                p.mmacc(pdst[:, hi * 64:(hi + 1) * 64],
                        [(cfm[0:64, t0:t0 + 128], hin_b[0:64, h * 64:(h + 1) * 64])], mm)
            p.wait("v", mm)
            p.wait("v", sc)
            for hi in range(16):
                h = hh0 + hi
                pdst = psy1 if wave == 0 else psy2
                p.step("v", lambda E, o=y_t[:, h * 64:(h + 1) * 64],
                       i0=pdst[:, hi * 64:(hi + 1) * 64], s_=e2p[0:128, h:h + 1],
                       i1=y1str[:, buf * DIN + h * 64:buf * DIN + (h + 1) * 64]:
                       E.scalar_tensor_tensor(o, i0, s_, i1, ALU.mult, ALU.add), ev)
            p.wait("t", ev)
        p.wait("s", ev)
        p.step("s", lambda E, o=zsil[:, :], i_=zstr[:, buf * DIN:(buf + 1) * DIN]:
               E.activation(o, i_, AF.Silu), sc)
        p.wait("v", sc)
        p.step("v", lambda E, o=u_t[:, :], a=y_t[:, :], b=zsil[:, :]: E.tensor_mul(o, a, b), ev)
        p.wait("s", ev)
        p.wait("v", ev)
        emit_rms(p, u_t[:, :], un_t[:, :], scratch8, stats, sc, ev, DIN, fcon)
        p.wait("t", ev)
        for k in range(16):
            p.step("t", lambda E, i_=un_t[:, k * 128:(k + 1) * 128], o=pst_b[:, 0:128], idn=idb:
                   E.transpose(o, i_, idn), mm)
            p.wait("v", mm)
            p.step("v", lambda E, o=u_fm[:, k * 128:(k + 1) * 128], i_=pst_b[:, 0:128]:
                   E.tensor_copy(o, i_), ev)
            p.wait("t", ev)
        for nn in range(2):
            p.mmacc(pso[:, nn * 512:(nn + 1) * 512],
                    [(u_fm[:, k * 128:(k + 1) * 128], wouts[:, k * D + nn * 512:k * D + (nn + 1) * 512])
                     for k in range(16)], mm)
        p.wait("s", ev)
        lnx = scratch8[:, 0:1024]
        emit_ln(p, xstr[:, buf * D:(buf + 1) * D], lnx, x1st[:, 1024:2048].tensor if False else x1st,
                stats, sc, ev, epscol=fcon)
        p.wait("v", ev)
        p.wait("v", mm)
        p.step("v", lambda E, o=x1st[:, 0:512], a=pso[:, 0:512],
               b=scratch8[:, 0:512]: E.tensor_add(o, a, b), ev)
        p.step("v", lambda E, o=x1st[:, 512:1024], a=pso[:, 512:1024],
               b=scratch8[:, 512:1024]: E.tensor_add(o, a, b), ev)
        p.wait("v", ev)
        p.step("v", lambda E, o=x1st[:, 1024:2048], a=x1st[:, 0:1024],
               b=xstr[:, buf * D:(buf + 1) * D]: E.tensor_add(o, a, b), ev)
        p.wait("t", ev)
        p.wait("y", ev)
        p.dma("y", x1_scr[t0:t0 + 128, :], x1st[:, 0:1024], st)
        p.dma("y", xp_scr[t0:t0 + 128, :], x1st[:, 1024:2048], st)
        p.wait("y", ev)
        p.wait("v", st, st.n)
    p.barrier("t", "v", "s", "y", "g")
    if upto <= 2:
        p.dma("y", o_x1[:, :], x1_scr[:, :], st)
        p.wait("y", st)
        return p.finish()

    for i in range(NTILE):
        buf = i % 2
        p.dma("y", xstr[:, buf * D:(buf + 1) * D], x1_scr[i * 128:(i + 1) * 128, :], ld)
        p.wait("s", ld)
        p.wait("v", ld)
        emit_rms(p, xstr[:, buf * D:(buf + 1) * D], hnt[:, 0:D], scratch8, stats, sc, ev, D, fcon)
        p.wait("t", ev)
        for k in range(8):
            p.step("t", lambda E, i_=hnt[:, k * 128:(k + 1) * 128], o=pst_b[:, 0:128], idn=idb:
                   E.transpose(o, i_, idn), mm)
            p.wait("v", mm)
            p.step("v", lambda E, o=hnf[:, k * CT + i * 128:k * CT + (i + 1) * 128],
                   i_=pst_b[:, 0:128]: E.tensor_copy(o, i_), ev)
            p.wait("t", ev)
        p.wait("y", ev)
    p.barrier("t", "v", "s", "y", "g")
    for k in range(8):
        p.step("v", lambda E, o=hnf[:, k * CT:(k + 1) * CT], i_=hnf[:, k * CT:(k + 1) * CT],
               s_=rwcol[:, k:k + 1]: E.tensor_scalar_mul(o, i_, s_), ev)
    p.barrier("v", "t", "g")

    for m in range(32):
        for k in range(8):
            p.dma("g", wstr[:, k * 128:(k + 1) * 128],
                  mlp_w1[k * 128:(k + 1) * 128, m * 128:(m + 1) * 128], wv)
        p.wait("t", wv)
        for n in range(4):
            pb = (ps1, ps2)[n % 2]
            p.mmacc(pb[:, :],
                    [(wstr[:, k * 128:(k + 1) * 128], hnf[:, k * CT + n * 512:k * CT + (n + 1) * 512])
                     for k in range(8)], mm)
            p.wait("s", mm)
            p.step("s", lambda E, o=instr_s[:, (n % 2) * 512:(n % 2 + 1) * 512],
                   i_=pb[:, :], b=b1c[:, m:m + 1]: E.activation(o, i_, AF.Silu, bias=b), sc)
            p.wait("y", sc)
            p.dma("y", in_scr[m, :, n * 512:(n + 1) * 512], instr_s[:, (n % 2) * 512:(n % 2 + 1) * 512], st)
            p.wait("t", sc, sc.n - 1)
            p.wait("s", st, st.n - 16)
        p.wait("g", sc, sc.n)
    p.barrier("t", "v", "s", "y", "g")
    if upto <= 3:
        for m_ in range(16):
            p.dma("g", o_xn[m_ * 128:(m_ + 1) * 128, :], in_scr[m_, :, 0:1024], st)
        p.wait("y", st)
        p.wait("g", st)
        return p.finish()

    p.wait("g", st)
    if upto == 39:
        for m_ in range(8):
            p.dma("g", instr_s[:, 0:512], in_scr[m_, :, 0:512], wv)
            p.wait("v", wv)
            p.step("v", lambda E, o=scratch8[:, 0:512], i_=instr_s[:, 0:512]: E.tensor_copy(o, i_), ev)
            p.wait("y", ev)
            p.dma("y", o_xn[m_ * 128:(m_ + 1) * 128, 0:512], scratch8[:, 0:512], st)
            p.wait("g", st)
        p.wait("y", st)
        return p.finish()
    for n in range(4 if upto != 4 else DBG_N):
        tg = n * 512
        for dhalf in range(2 if upto != 4 else DBG_DH):
            for k in range(32):
                kb = k % 2
                p.dma("g", instr_s[:, kb * 512:(kb + 1) * 512], in_scr[k, :, tg:tg + 512], wv)
                p.dma("g", w2str[:, kb * D:kb * D + 512],
                      mlp_w2[k * 128:(k + 1) * 128, dhalf * 512:(dhalf + 1) * 512], wv)
                p.wait("t", wv)
                for dq in range(4):
                    pb = (psy1, psy2)[dq // 2]
                    fin = (dq == 3)
                    p.step("t", lambda E, o=pb[:, (dq % 2) * 512:(dq % 2 + 1) * 512],
                           l=w2str[:, kb * D + dq * 128:kb * D + (dq + 1) * 128],
                           r=instr_s[:, kb * 512:(kb + 1) * 512],
                           st_=(k == 0): E.matmul(o, l, r, start=st_, stop=True),
                           mm if fin else None)
                p.wait("g", mm, mm.n)
            p.wait("v", mm)
            for dq in range(4):
                d_ = dhalf * 4 + dq
                pb = (psy1, psy2)[dq // 2]
                p.step("v", lambda E, o=mo_fm[:, d_ * 512:(d_ + 1) * 512],
                       i_=pb[:, (dq % 2) * 512:(dq % 2 + 1) * 512], b=b2c[:, d_:d_ + 1]:
                       E.tensor_scalar_add(o, i_, b), ev)
            p.wait("t", ev)
        if upto == 40 and n == 0:
            p.wait("g", ev)
            for q_ in range(8):
                p.dma("g", o_xn[q_ * 128:(q_ + 1) * 128, 0:512], mo_fm[:, q_ * 512:(q_ + 1) * 512], st)
            p.wait("g", st)
        for sub in range(4 if upto >= 41 else 0):
            ti = n * 4 + sub
            t0 = ti * 128
            buf = sub % 2
            p.dma("y", xstr[:, buf * D:(buf + 1) * D], xp_scr[t0:t0 + 128, :], ld)
            for d_ in range(8):
                p.step("t", lambda E, i_=mo_fm[:, d_ * 512 + sub * 128:d_ * 512 + (sub + 1) * 128],
                       o=pst_b[:, 0:128], idn=idb: E.transpose(o, i_, idn), mm)
                p.wait("v", mm)
                p.step("v", lambda E, o=mo_tm[:, d_ * 128:(d_ + 1) * 128], i_=pst_b[:, 0:128]:
                       E.tensor_copy(o, i_), ev)
                p.wait("t", ev)
            p.wait("v", ld)
            p.step("v", lambda E, o=xstr[:, buf * D:(buf + 1) * D], a=xstr[:, buf * D:(buf + 1) * D],
                   b=mo_tm[:, :]: E.tensor_add(o, a, b), ev)
            p.wait("v", ev)
            p.wait("y", ev)
            p.dma("y", o_x1[t0:t0 + 128, :], xstr[:, buf * D:(buf + 1) * D], st)
            if upto <= 41:
                p.wait("v", st, st.n)
                continue
            p.wait("s", ev)
            emit_ln(p, xstr[:, buf * D:(buf + 1) * D], xn_t[:, :], scratch8, stats, sc, ev,
                    epscol=fcon, extra_outs=(xn_b[:, :],))
            p.wait("y", ev)
            p.dma("y", o_xn[t0:t0 + 128, :], xn_t[:, :], st)
            if upto <= 5:
                p.wait("v", st, st.n)
                continue
            p.wait("t", ev)
            for k in range(8):
                p.step("t", lambda E, i_=xn_b[:, k * 128:(k + 1) * 128], o=pst_b[:, 0:128], idn=idb:
                       E.transpose(o, i_, idn), mm)
                p.wait("v", mm)
                p.step("v", lambda E, o=xnfr[:, k * 128:(k + 1) * 128], i_=pst_b[:, 0:128]:
                       E.tensor_copy(o, i_), ev)
                p.wait("t", ev)
            p.mmacc(ps1[:, 0:8],
                    [(xnfr[:, k * 128:(k + 1) * 128], gwt[:, k * 8:(k + 1) * 8]) for k in range(8)], mm)
            p.wait("v", mm)
            p.step("v", lambda E, o=rlog[:, :], i_=ps1[:, 0:8]: E.tensor_copy(o, i_), ev)
            p.wait("t", ev)
            p.wait("v", ev)
            p.step("v", lambda E, o=rstat[:, 0:1], i_=rlog[:, :]: E.reduce_max(o, i_, axis=AX.X), ev)
            p.wait("v", ev)
            p.step("v", lambda E, o=rstat[:, 1:2], i_=rstat[:, 0:1]: E.tensor_scalar_mul(o, i_, -1.0), ev)
            p.wait("s", ev)
            p.step("s", lambda E, o=rprob[:, :], i_=rlog[:, :], b=rstat[:, 1:2], acc=rstat[:, 2:3]:
                   E.activation(o, i_, AF.Exp, bias=b, accum_out=acc), sc)
            p.wait("v", sc)
            p.step("v", lambda E, o=rstat[:, 3:4], i_=rstat[:, 2:3]: E.reciprocal(o, i_), ev)
            p.wait("v", ev)
            p.step("v", lambda E, o=rprob[:, :], i_=rprob[:, :], s_=rstat[:, 3:4]:
                   E.tensor_scalar_mul(o, i_, s_), ev)
            p.wait("v", ev)
            p.step("v", lambda E, o=rstat[:, 4:5], i_=rprob[:, :]: E.reduce_max(o, i_, axis=AX.X), ev)
            p.wait("v", ev)
            p.step("v", lambda E, o=rm1[:, :], i_=rprob[:, :], s_=rstat[:, 4:5]:
                   E.tensor_single_scalar(o, i_, s_, ALU.is_ge), ev)
            p.wait("v", ev)
            p.step("v", lambda E, o=rmsum[:, :], i0=rm1[:, :], i1=rprob[:, :]:
                   E.scalar_tensor_tensor(o, i0, -1e30, i1, ALU.mult, ALU.add), ev)
            p.wait("v", ev)
            p.step("v", lambda E, o=rstat[:, 5:6], i_=rmsum[:, :]: E.reduce_max(o, i_, axis=AX.X), ev)
            p.wait("v", ev)
            p.step("v", lambda E, o=rm2[:, :], i_=rmsum[:, :], s_=rstat[:, 5:6]:
                   E.tensor_single_scalar(o, i_, s_, ALU.is_ge), ev)
            p.wait("v", ev)
            p.step("v", lambda E, o=rmsum[:, :], a=rm1[:, :], b=rm2[:, :]: E.tensor_add(o, a, b), ev)
            p.wait("t", ev)
            p.mmacc(ps1[:, 0:8],
                    [(cume[:, :], rmsum[:, :]), (onesf[0:1, 0:128], offr[0:1, :])], mm)
            p.mmacc(ps2[0:1, 0:8], [(onesf[:, 0:1], rmsum[:, :])], mm)
            p.wait("v", mm)
            p.step("v", lambda E, o=rc1[:, :], a=ps1[:, 0:8], b=rm1[:, :]: E.tensor_add(o, a, b), ev)
            p.wait("v", ev)
            p.step("v", lambda E, o=rc2[:, :], a=rc1[:, :], b=rm2[:, :]: E.tensor_add(o, a, b), ev)
            p.step("v", lambda E, o=offr[0:1, :], a=ps2[0:1, 0:8], b=offr[0:1, :]:
                   E.tensor_add(o, a, b), ev)
            p.wait("t", ev)
            p.wait("y", ev)
            p.dma("y", o_m1[t0:t0 + 128, :], rm1[:, :], st)
            p.dma("y", o_m2[t0:t0 + 128, :], rm2[:, :], st)
            p.dma("y", o_probs[t0:t0 + 128, :], rprob[:, :], st)
            p.dma("y", o_c1[t0:t0 + 128, :], rc1[:, :], st)
            p.dma("y", o_c2[t0:t0 + 128, :], rc2[:, :], st)
            p.wait("v", st, st.n)
    p.dma("y", o_tot[:, :], offr[0:1, :], st)
    p.wait("y", st)
    return p.finish()


def run_A2(inputs, a1_results, upto=99):
    consts = const_inputs(["ident_b", "ident_f", "cum_incl", "cum_excl", "ones_f", "erep32"])
    x = np.asarray(inputs["L0_attn_hidden_tokens"])
    hloc_all = np.stack([np.asarray(a1_results[i]["hloc"]) for i in range(NCORE)])
    lsum_all = np.stack([np.asarray(a1_results[i]["lsum"]).reshape(-1) for i in range(NCORE)])
    in_maps = []
    for c in range(NCORE):
        b, j = c // 4, c % 4
        cmask = np.zeros((NCORE, NCORE), np.float32)
        cvalid = np.zeros((NCORE, 1), np.float32)
        for i in range(4 * b, c):
            cvalid[i, 0] = 1.0
            for k in range(i + 1, c):
                cmask[k, i] = 1.0
        m = {
            "x": np.ascontiguousarray(x[b, j * CT:(j + 1) * CT]),
            "y1": np.asarray(a1_results[c]["y1"]),
            "z": np.asarray(a1_results[c]["z"]),
            "cfm": np.asarray(a1_results[c]["cfm"]),
            "logda": np.asarray(a1_results[c]["logda"]),
            "hloc_all": hloc_all, "lsum_all": lsum_all,
            "cmask": cmask, "cvalid": cvalid,
            "w_out": np.asarray(inputs["L0_attn_W_out"]),
            "norm_w": np.asarray(inputs["L0_attn_norm_w"]).reshape(-1, 1),
            "rms_w": np.asarray(inputs["L0_attn_rms_w"]).reshape(-1, 1),
            "mlp_w1": np.asarray(inputs["L0_attn_mlp_w1"]),
            "mlp_b1": np.ascontiguousarray(np.asarray(inputs["L0_attn_mlp_b1"]).reshape(32, 128).T),
            "mlp_w2": np.asarray(inputs["L0_attn_mlp_w2"]),
            "mlp_b2": np.ascontiguousarray(np.asarray(inputs["L0_attn_mlp_b2"]).reshape(8, 128).T),
            "gate_wt": np.ascontiguousarray(np.asarray(inputs["L0_ffn_gate_w"]).T),
        }
        m.update(consts)
        in_maps.append(m)
    nc = build_A2(upto)
    res = run_bass_kernel_spmd(nc, in_maps, core_ids=list(range(NCORE)))
    return res.results



def build_A3():
    p = Prog()
    CAP = float(int(S * 2 / 8 * 1.25))
    x1 = p.inp("x1", [CT, D])
    xn = p.inp("xn", [CT, D])
    m1_i = p.inp("m1", [CT, 8])
    m2_i = p.inp("m2", [CT, 8])
    probs_i = p.inp("probs", [CT, 8])
    c1_i = p.inp("c1", [CT, 8])
    c2_i = p.inp("c2", [CT, 8])
    tots = p.inp("tots", [NCORE, 8])
    bmask = p.inp("bmask", [NCORE, 1])
    w1 = p.inp("w1", [8, D, 3584])
    w2 = p.inp("w2", [8, 3584, D])
    w3 = p.inp("w3", [8, D, 3584])
    ln_scale = p.inp("ln_scale", [1, D])
    ln_bias = p.inp("ln_bias", [1, D])
    consts, load_consts = use_consts(p, ["ident_b", "ones_f"])

    o_x2 = p.outp("x2", [CT, D])
    o_z0 = p.outp("z0", [CT, D])
    o_znf = p.outp("znf", [D, CT], BF16)

    act_scr = p.nc.dram_tensor("act_scr", [28, 128, CT], BF16).ap()

    xnf = p.sbuf("xnf", [128, 8 * CT], BF16)
    y_acc = p.sbuf("y_acc", [128, NTILE * D], F32)
    gates = p.sbuf("gates", [128, NTILE * 8], F32)
    xstr = p.sbuf("xstr", [128, 2 * D], F32)
    xn_b = p.sbuf("xn_b", [128, D], BF16)
    w13 = p.sbuf("w13", [128, 2 * 8 * 128], BF16)
    w2str = p.sbuf("w2str", [128, D], BF16)
    hsil = p.sbuf("hsil", [128, 512], BF16)
    actstg = p.sbuf("actstg", [128, 512], BF16)
    actld = p.sbuf("actld", [128, 512], BF16)
    totss = p.sbuf("totss", [NCORE, 8], F32)
    bmsk = p.sbuf("bmsk", [NCORE, 1], F32)
    offs = p.sbuf("offs", [1, 8], F32)
    offb = p.sbuf("offb", [128, 8], F32)
    gtmp = p.sbuf("gtmp", [128, 8 * 4], F32)
    scrow = p.sbuf("scrow", [1, D], F32)
    bcrow = p.sbuf("bcrow", [1, D], F32)
    scb = p.sbuf("scb", [128, D], F32)
    bsb = p.sbuf("bsb", [128, D], F32)
    z0_t = p.sbuf("z0_t", [128, D], F32)
    zn_t = p.sbuf("zn_t", [128, D], F32)
    zn_b = p.sbuf("zn_b", [128, D], BF16)
    znstg = p.sbuf("znstg", [128, D], BF16)
    dtmp = p.sbuf("dtmp", [128, 512], F32)
    scratch8 = p.sbuf("scratch8", [128, 2048], F32)
    stats = p.sbuf("stats", [128, 8], F32)
    fcon = p.sbuf("fcon", [128, 2], F32)

    ps1 = p.psum("ps1", [128, 512])
    ps2 = p.psum("ps2", [128, 512])
    ps3 = p.psum("ps3", [128, 512])
    psbT = p.psum("psbT", [128, 1024], BF16)
    psd = p.psum("psd", [128, 2048])
    psd2 = psd

    ld = p.sem("ld")
    wv = p.sem("wv")
    mm = p.sem("mm")
    ev = p.sem("ev")
    sc = p.sem("sc")
    st = p.sem("st")

    idb = consts["ident_b"][:, :]
    onesf = consts["ones_f"]

    load_consts(ld)
    p.dma("y", totss[0:NCORE, :], tots[:, :], ld)
    p.dma("y", bmsk[0:NCORE, :], bmask[:, :], ld)
    p.dma("y", scrow[0:1, :], ln_scale[:, :], ld)
    p.dma("y", bcrow[0:1, :], ln_bias[:, :], ld)
    p.op("g", lambda E: E.memset(fcon[:, 0:1], EPS))
    p.op("g", lambda E: E.memset(fcon[:, 1:2], 1.0))
    p.op("v", lambda E: E.memset(y_acc[:, :], 0.0))
    p.barrier("y", "g", "s", "v", "t")
    p.settle(ld, wv)

    p.mmacc(ps1[0:1, 0:8], [(bmsk[0:NCORE, :], totss[0:NCORE, :])], mm)
    p.wait("v", mm)
    p.step("v", lambda E, o=offs[0:1, :], i_=ps1[0:1, 0:8]: E.tensor_copy(o, i_), ev)
    p.wait("t", ev)
    p.mmacc(ps1[0:128, 0:8], [(onesf[0:1, :], offs[0:1, :])], mm)
    p.wait("v", mm)
    p.step("v", lambda E, o=offb[:, :], i_=ps1[0:128, 0:8]: E.tensor_copy(o, i_), ev)
    p.wait("t", ev)

    for h_ in range(2):
        p.mmacc(ps1[:, :], [(onesf[0:1, :], scrow[0:1, h_ * 512:(h_ + 1) * 512])], mm)
        p.mmacc(ps2[:, :], [(onesf[0:1, :], bcrow[0:1, h_ * 512:(h_ + 1) * 512])], mm)
        p.wait("v", mm)
        p.step("v", lambda E, o=scb[:, h_ * 512:(h_ + 1) * 512], i_=ps1[:, :]: E.tensor_copy(o, i_), ev)
        p.step("v", lambda E, o=bsb[:, h_ * 512:(h_ + 1) * 512], i_=ps2[:, :]: E.tensor_copy(o, i_), ev)
        p.wait("t", ev)

    for i in range(NTILE):
        buf = i % 2
        p.dma("y", xstr[:, buf * D:(buf + 1) * D], xn[i * 128:(i + 1) * 128, :], ld)
        p.dma("y", gtmp[:, 0:8], c1_i[i * 128:(i + 1) * 128, :], ld)
        p.dma("y", gtmp[:, 8:16], c2_i[i * 128:(i + 1) * 128, :], ld)
        p.dma("y", gtmp[:, 16:24], m1_i[i * 128:(i + 1) * 128, :], ld)
        p.dma("y", gtmp[:, 24:32], m2_i[i * 128:(i + 1) * 128, :], ld)
        p.dma("y", gates[:, i * 8:(i + 1) * 8], probs_i[i * 128:(i + 1) * 128, :], ld)
        p.wait("v", ld)
        p.step("v", lambda E, o=gtmp[:, 0:8], a=gtmp[:, 0:8], b=offb[:, :]: E.tensor_add(o, a, b), ev)
        p.step("v", lambda E, o=gtmp[:, 8:16], a=gtmp[:, 8:16], b=offb[:, :]: E.tensor_add(o, a, b), ev)
        p.wait("v", ev)
        p.step("v", lambda E, o=gtmp[:, 0:8], i_=gtmp[:, 0:8], c=CAP:
               E.tensor_single_scalar(o, i_, c, ALU.is_le), ev)
        p.step("v", lambda E, o=gtmp[:, 8:16], i_=gtmp[:, 8:16], c=CAP:
               E.tensor_single_scalar(o, i_, c, ALU.is_le), ev)
        p.wait("v", ev)
        p.step("v", lambda E, o=gtmp[:, 0:8], a=gtmp[:, 0:8], b=gtmp[:, 16:24]: E.tensor_mul(o, a, b), ev)
        p.step("v", lambda E, o=gtmp[:, 8:16], a=gtmp[:, 8:16], b=gtmp[:, 24:32]: E.tensor_mul(o, a, b), ev)
        p.wait("v", ev)
        p.step("v", lambda E, o=gtmp[:, 0:8], a=gtmp[:, 0:8], b=gtmp[:, 8:16]: E.tensor_add(o, a, b), ev)
        p.wait("v", ev)
        p.step("v", lambda E, o=gates[:, i * 8:(i + 1) * 8], a=gates[:, i * 8:(i + 1) * 8],
               b=gtmp[:, 0:8]: E.tensor_mul(o, a, b), ev)
        p.step("v", lambda E, o=xn_b[:, :], i_=xstr[:, buf * D:(buf + 1) * D]: E.tensor_copy(o, i_), ev)
        p.wait("t", ev)
        for k in range(8):
            p.step("t", lambda E, i_=xn_b[:, k * 128:(k + 1) * 128], o=psbT[:, 0:128], idn=idb:
                   E.transpose(o, i_, idn), mm)
            p.wait("v", mm)
            p.step("v", lambda E, o=xnf[:, k * CT + i * 128:k * CT + (i + 1) * 128],
                   i_=psbT[:, 0:128]: E.tensor_copy(o, i_), ev)
            p.wait("t", ev)
        p.wait("y", ev)
    p.barrier("t", "v", "s", "y", "g")

    for e in range(8):
        p.wait("y", wv, wv.n)
        for mi in range(28):
            for k in range(8):
                p.dma("g", w13[:, k * 128:(k + 1) * 128],
                      w1[e, k * 128:(k + 1) * 128, mi * 128:(mi + 1) * 128], wv)
                p.dma("g", w13[:, 1024 + k * 128:1024 + (k + 1) * 128],
                      w3[e, k * 128:(k + 1) * 128, mi * 128:(mi + 1) * 128], wv)
            p.wait("t", wv)
            for n in range(4):
                p.mmacc(ps1[:, :], [(w13[:, k * 128:(k + 1) * 128],
                                     xnf[:, k * CT + n * 512:k * CT + (n + 1) * 512])
                                    for k in range(8)], mm)
                p.mmacc(ps2[:, :], [(w13[:, 1024 + k * 128:1024 + (k + 1) * 128],
                                     xnf[:, k * CT + n * 512:k * CT + (n + 1) * 512])
                                    for k in range(8)], mm)
                p.wait("s", mm)
                p.step("s", lambda E, o=hsil[:, :], i_=ps1[:, :]: E.activation(o, i_, AF.Silu), sc)
                p.wait("v", sc)
                p.step("v", lambda E, o=actstg[:, :], a=hsil[:, :], b=ps2[:, :]: E.tensor_mul(o, a, b), ev)
                p.wait("t", ev)
                p.wait("y", ev)
                p.dma("y", act_scr[mi, :, n * 512:(n + 1) * 512], actstg[:, :], st)
                p.wait("v", st, st.n - 16)
            p.wait("g", mm, mm.n)
        p.wait("g", st, st.n)
        for n in range(4):
          for tw in range(2):
            for mi in range(28):
                p.dma("g", actld[:, :], act_scr[mi, :, n * 512:(n + 1) * 512], wv)
                p.dma("g", w2str[:, :], w2[e, mi * 128:(mi + 1) * 128, :], wv)
                p.wait("t", wv)
                for tsub in range(2):
                    tg2 = tw * 2 + tsub
                    off = tsub * 1024
                    for dc in range(2):
                        fin = (tsub == 1 and dc == 1)
                        p.step("t", lambda E, o=psd[:, off + dc * 512:off + (dc + 1) * 512],
                               l=actld[:, tg2 * 128:(tg2 + 1) * 128],
                               r=w2str[:, dc * 512:(dc + 1) * 512],
                               st_=(mi == 0): E.matmul(o, l, r, start=st_, stop=True),
                               mm if fin else None)
                p.wait("g", mm, mm.n)
            p.wait("v", mm)
            for tsub in range(2):
                ti = n * 4 + tw * 2 + tsub
                off = tsub * 1024
                for dc in range(2):
                    p.step("v", lambda E, o=dtmp[:, :], i_=psd[:, off + dc * 512:off + (dc + 1) * 512],
                           s_=gates[:, ti * 8 + e:ti * 8 + e + 1]: E.tensor_scalar_mul(o, i_, s_), ev)
                    p.wait("v", ev)
                    p.step("v", lambda E, o=y_acc[:, ti * D + dc * 512:ti * D + (dc + 1) * 512],
                           a=y_acc[:, ti * D + dc * 512:ti * D + (dc + 1) * 512], b=dtmp[:, :]:
                           E.tensor_add(o, a, b), ev)
                    p.wait("v", ev)
            p.wait("t", ev)
    p.barrier("t", "v", "s", "y", "g")

    for i in range(NTILE):
        buf = i % 2
        p.dma("y", xstr[:, buf * D:(buf + 1) * D], x1[i * 128:(i + 1) * 128, :], ld)
        p.wait("v", ld)
        p.step("v", lambda E, o=xstr[:, buf * D:(buf + 1) * D], a=xstr[:, buf * D:(buf + 1) * D],
               b=y_acc[:, i * D:(i + 1) * D]: E.tensor_add(o, a, b), ev)
        p.wait("v", ev)
        p.wait("y", ev)
        p.dma("y", o_x2[i * 128:(i + 1) * 128, :], xstr[:, buf * D:(buf + 1) * D], st)
        p.wait("s", ev)
        emit_ln(p, xstr[:, buf * D:(buf + 1) * D], z0_t[:, :], scratch8, stats, sc, ev, epscol=fcon)
        p.wait("y", ev)
        p.dma("y", o_z0[i * 128:(i + 1) * 128, :], z0_t[:, :], st)
        p.wait("s", ev)
        emit_ln(p, z0_t[:, :], zn_t[:, :], scratch8, stats, sc, ev, epscol=fcon)
        p.wait("v", ev)
        p.step("v", lambda E, o=zn_t[:, :], a=zn_t[:, :], b=scb[:, :]: E.tensor_mul(o, a, b), ev)
        p.wait("v", ev)
        p.step("v", lambda E, o=zn_b[:, :], a=zn_t[:, :], b=bsb[:, :]: E.tensor_add(o, a, b), ev)
        p.wait("t", ev)
        for k in range(8):
            p.step("t", lambda E, i_=zn_b[:, k * 128:(k + 1) * 128], o=psbT[:, 0:128], idn=idb:
                   E.transpose(o, i_, idn), mm)
            p.wait("v", mm)
            p.step("v", lambda E, o=znstg[:, k * 128:(k + 1) * 128],
                   i_=psbT[:, 0:128]: E.tensor_copy(o, i_), ev)
            p.wait("t", ev)
        p.wait("y", ev)
        for k in range(8):
            p.dma("y", o_znf[k * 128:(k + 1) * 128, i * 128:(i + 1) * 128],
                  znstg[:, k * 128:(k + 1) * 128], st)
        p.wait("v", st, st.n - 16 * 8)
        p.wait("y", ev)
    p.wait("y", st)
    return p.finish()


def run_A3(inputs, a2_results):
    consts = const_inputs(["ident_b", "ones_f"])
    tots = np.concatenate([np.asarray(a2_results[i]["tot"]) for i in range(NCORE)], 0)
    in_maps = []
    for c in range(NCORE):
        b, j = c // 4, c % 4
        bm = np.zeros((NCORE, 1), np.float32)
        for i in range(4 * b, c):
            bm[i, 0] = 1.0
        m = {
            "x1": np.asarray(a2_results[c]["x1"]),
            "xn": np.asarray(a2_results[c]["xn"]),
            "m1": np.asarray(a2_results[c]["m1"]),
            "m2": np.asarray(a2_results[c]["m2"]),
            "probs": np.asarray(a2_results[c]["probs"]),
            "c1": np.asarray(a2_results[c]["c1"]),
            "c2": np.asarray(a2_results[c]["c2"]),
            "tots": tots, "bmask": bm,
            "w1": np.asarray(inputs["L0_ffn_w1"]),
            "w2": np.asarray(inputs["L0_ffn_w2"]),
            "w3": np.asarray(inputs["L0_ffn_w3"]),
            "ln_scale": np.asarray(inputs["L1_attn_ln_scale"]).reshape(1, -1),
            "ln_bias": np.asarray(inputs["L1_attn_ln_bias"]).reshape(1, -1),
        }
        m.update(consts)
        in_maps.append(m)
    nc = build_A3()
    res = run_bass_kernel_spmd(nc, in_maps, core_ids=list(range(NCORE)))
    return res.results



def build_A4(rope=False, causal=False, SM_SCALE=0.125):
    p = Prog()
    znf = p.inp("znf", [4, D, CT], BF16)
    qw = p.inp("qw", [D, 256])
    kw = p.inp("kw", [D, 256])
    vw = p.inp("vw", [D, 256])
    qb = p.inp("qb", [256, 1])
    kb = p.inp("kb", [256, 1])
    vb = p.inp("vb", [256, 1])
    cl = ["ident_b", "ones_f", "hsel2", "bc2"] + (["cmask4"] if causal else []) + (["swap64", "signcol"] if rope else [])
    consts, load_consts = use_consts(p, cl)
    if rope:
        theta = p.inp("theta", [128, 1])
        trow = p.inp("trow", [1, S])
    o_ctx = p.outp("ctx", [256, S], BF16)

    NKT = S // 128
    NQ = S // 512

    qkv = {}
    for nm in ("q", "k", "v"):
        for pr in range(2):
            qkv[(nm, pr)] = p.sbuf(f"{nm}{pr}", [128, S], BF16)
    vtm = [p.sbuf(f"vtm{h}", [128, NKT * 66], BF16) for h in range(4)]
    wq = p.sbuf("wq", [128, 8 * 768], BF16)
    bcol = p.sbuf("bcol", [128, 6], F32)
    zbuf = p.sbuf("zbuf", [128, 8 * 512], BF16)
    ptile = p.sbuf("ptile", [128, 8 * 512], BF16)
    ctx_sb = p.sbuf("ctx_sb", [128, 2 * 512], F32)
    kap = p.sbuf("kap", [128, 16], F32)
    mrow = p.sbuf("mrow", [2, 8], F32)
    mbias = p.sbuf("mbias", [128, 4], F32)
    sq = p.sbuf("sq", [128, 512], F32)
    ctxo = p.sbuf("ctxo", [128, 2 * 512], BF16)
    if rope:
        cosb = p.sbuf("cosb", [128, S], BF16)
        sinb = p.sbuf("sinb", [128, S], BF16)
        ang = p.sbuf("ang", [128, 512], F32)
        rtmp = sq[:, 0:512]
        rtm2 = ctx_sb[:, 0:512]
        trows = p.sbuf("trows", [1, 512], F32)
        thcol = p.sbuf("thcol", [128, 1], F32)

    ps1 = p.psum("ps1", [128, 512])
    ps2 = p.psum("ps2", [128, 512])
    ps3 = p.psum("ps3", [128, 512])
    ps4 = p.psum("ps4", [128, 512])
    psc1 = p.psum("psc1", [128, 512])
    psc2 = p.psum("psc2", [128, 512])
    psx = p.psum("psx", [128, 512])
    psbT = p.psum("psbT", [128, 1024], BF16)

    ld = p.sem("ld")
    wv = p.sem("wv")
    mm = p.sem("mm")
    ev = p.sem("ev")
    sc = p.sem("sc")
    st = p.sem("st")

    idb = consts["ident_b"][:, :]
    onesf = consts["ones_f"]
    hsel2 = consts["hsel2"]
    bc2 = consts["bc2"]

    load_consts(ld)
    for k in range(8):
        p.dma("g", wq[:, k * 768:k * 768 + 256], qw[k * 128:(k + 1) * 128, :], wv)
        p.dma("g", wq[:, k * 768 + 256:k * 768 + 512], kw[k * 128:(k + 1) * 128, :], wv)
        p.dma("g", wq[:, k * 768 + 512:k * 768 + 768], vw[k * 128:(k + 1) * 128, :], wv)
    p.dma("y", bcol[:, 0:1], qb[0:128, :], ld)
    p.dma("y", bcol[:, 1:2], qb[128:256, :], ld)
    p.dma("y", bcol[:, 2:3], kb[0:128, :], ld)
    p.dma("y", bcol[:, 3:4], kb[128:256, :], ld)
    p.dma("y", bcol[:, 4:5], vb[0:128, :], ld)
    p.dma("y", bcol[:, 5:6], vb[128:256, :], ld)
    if rope:
        p.dma("y", thcol[:, :], theta[:, :], ld)
    p.barrier("y", "g", "s", "v", "t")
    p.settle(ld, wv)

    if rope:
        TWO_PI = 6.283185307179586
        for nn in range(S // 512):
            s0 = nn * 512
            if True:
                p.wait("y", mm, mm.n)
                p.dma("y", trows[0:1, :], trow[0:1, s0:s0 + 512], ld)
                p.wait("t", ld)
                p.mmacc(ps1[:, :], [(onesf[0:1, :], trows[0:1, 0:512])], mm)
                p.wait("v", mm)
                p.step("v", lambda E, o=ang[:, :], i_=ps1[:, :],
                       s_=thcol[:, 0:1]: E.tensor_scalar_mul(o, i_, s_), ev)
                p.wait("t", ev)
            for (dst, shift) in ((cosb, 0.25), (sinb, 0.0)):
                p.wait("v", ev)
                p.step("v", lambda E, o=rtmp[:, :], i_=ang[:, :], k=1.0 / TWO_PI, b=shift:
                       E.tensor_scalar(o, i_, k, b, ALU.mult, ALU.add), ev)
                p.wait("v", ev)
                p.step("v", lambda E, o=rtm2[:, :], i_=rtmp[:, :], k=12582912.0:
                       E.tensor_scalar_add(o, i_, k), ev)
                p.wait("v", ev)
                p.step("v", lambda E, o=rtm2[:, :], i_=rtm2[:, :], k=-12582912.0:
                       E.tensor_scalar_add(o, i_, k), ev)
                p.wait("v", ev)
                p.step("v", lambda E, o=rtmp[:, :], a=rtmp[:, :], b=rtm2[:, :]:
                       E.tensor_sub(o, a, b), ev)
                p.wait("s", ev)
                p.step("s", lambda E, o=dst[:, s0:s0 + 512], i_=rtmp[:, :], k=TWO_PI:
                       E.activation(o, i_, AF.Sin, scale=k), sc)
                p.wait("v", sc)
        p.wait("t", sc)

    for r in range(4):
        for nn in range(4):
            for k in range(8):
                p.dma("y", zbuf[:, k * 512:(k + 1) * 512],
                      znf[r, k * 128:(k + 1) * 128, nn * 512:(nn + 1) * 512], ld)
            p.wait("t", ld)
            tc = r * CT + nn * 512
            for (mi, nm, pr) in ((0, "q", 0), (1, "q", 1), (2, "k", 0), (3, "k", 1),
                                 (4, "v", 0), (5, "v", 1)):
                pb = (ps1, ps2)[mi % 2]
                base = {"q": 0, "k": 256, "v": 512}[nm]
                p.mmacc(pb[:, :],
                        [(wq[:, k * 768 + base + pr * 128:k * 768 + base + (pr + 1) * 128],
                          zbuf[:, k * 512:(k + 1) * 512])
                         for k in range(8)], mm)
                p.wait("v", mm)
                bc = bcol[:, {"q": 0, "k": 2, "v": 4}[nm] + pr:{"q": 0, "k": 2, "v": 4}[nm] + pr + 1]
                p.step("v", lambda E, o=qkv[(nm, pr)][:, tc:tc + 512], i_=pb[:, :], b=bc:
                       E.tensor_scalar_add(o, i_, b), ev)
                p.wait("t", ev, ev.n - 1)
            p.wait("t", ev)
            p.wait("y", mm, mm.n)
    p.barrier("t", "v", "s", "y", "g")

    if rope:
        swp = consts["swap64"]
        p.wait("v", sc)
        for nn in range(S // 2048):
            p.step("v", lambda E, o=sinb[:, nn * 2048:(nn + 1) * 2048],
                   i_=sinb[:, nn * 2048:(nn + 1) * 2048],
                   s_=consts["signcol"][:, 0:1]: E.tensor_scalar_mul(o, i_, s_), ev)
        p.wait("t", ev)

        for nm in ("q", "k"):
            for pr in range(2):
                buf = qkv[(nm, pr)]
                for base in (0, 64):
                    rr = slice(base, base + 64)
                    for nn in range(S // 512):
                        ss = slice(nn * 512, (nn + 1) * 512)
                        pb = ps1 if base == 0 else ps2
                        swb = ptile[rr, 0:512]
                        t1b = ptile[rr, 512:1024]
                        p.mmacc(pb[rr, :], [(swp[rr, 0:64], buf[rr, ss])], mm)
                        p.wait("v", mm)
                        p.step("v", lambda E, o=swb, i_=pb[rr, :]: E.tensor_copy(o, i_), ev)
                        p.wait("v", ev)
                        p.step("v", lambda E, o=t1b, a=buf[rr, ss], b=cosb[rr, ss]:
                               E.tensor_mul(o, a, b), ev)
                        p.wait("v", ev)
                        p.step("v", lambda E, o=swb, a=swb, b=sinb[rr, ss]:
                               E.tensor_mul(o, a, b), ev)
                        p.wait("v", ev)
                        p.step("v", lambda E, o=buf[rr, ss], a=t1b, b=swb:
                               E.tensor_add(o, a, b), ev)
                        p.wait("t", ev)
        p.barrier("v", "t", "s")

    for h in range(4):
        pr, sub = h // 2, (h % 2) * 64
        vb_ = qkv[("v", pr)]
        p.op("v", lambda E, o=vtm[h][:, :]: E.memset(o, 0.0))
        p.step("v", lambda E: E.nop(), ev)
        p.wait("v", ev)
        for kt in range(NKT):
            p.step("t", lambda E, i_=vb_[sub:sub + 64, kt * 128:(kt + 1) * 128],
                   o=psbT[:, 0:64], idn=idb[sub:sub + 64, sub:sub + 64]:
                   E.transpose(o, i_, idn), mm)
            p.wait("v", mm)
            p.step("v", lambda E, o=vtm[h][:, kt * 66:kt * 66 + 64], i_=psbT[:, 0:64]:
                   E.tensor_copy(o, i_), ev)
            p.wait("t", ev)
        p.step("v", lambda E, o=bass.AP(vtm[h], 64, [[NKT * 66, 128], [66, NKT], [1, 1]]):
               E.memset(o, 1.0), ev)
    p.barrier("t", "v", "s")

    p.op("v", lambda E: E.memset(kap[0:2, :], 0.0))
    p.step("v", lambda E: E.nop(), ev)
    for pr in range(2):
        for nm, col in (("q", 0), ("k", 1)):
            buf = qkv[(nm, pr)]
            for nn in range(NQ):
                p.wait("s", ev)
                p.step("s", lambda E, o=sq[:, :], i_=buf[:, nn * 512:(nn + 1) * 512]:
                       E.activation(o, i_, AF.Square), sc)
                p.wait("t", sc)
                p.mmacc(psx[0:2, :], [(hsel2[0:128, 0:2], sq[:, :])], mm)
                p.wait("v", mm)
                p.step("v", lambda E, o=kap[0:2, 8 + col:8 + col + 1], i_=psx[0:2, :]:
                       E.reduce_max(o, i_, axis=AX.X), ev)
                p.wait("v", ev)
                p.step("v", lambda E, o=kap[0:2, pr * 2 + col:pr * 2 + col + 1],
                       a=kap[0:2, pr * 2 + col:pr * 2 + col + 1],
                       b=kap[0:2, 8 + col:8 + col + 1]: E.tensor_max(o, a, b), ev)
                p.wait("t", ev)
                p.wait("s", ev)
    for pr in range(2):
        p.wait("v", ev)
        p.step("v", lambda E, o=mrow[0:2, pr:pr + 1], a=kap[0:2, pr * 2:pr * 2 + 1],
               b=kap[0:2, pr * 2 + 1:pr * 2 + 2]: E.tensor_mul(o, a, b), ev)
    p.wait("s", ev)
    for pr in range(2):
        p.step("s", lambda E, o=mrow[0:2, 2 + pr:3 + pr], i_=mrow[0:2, pr:pr + 1]:
               E.activation(o, i_, AF.Sqrt), sc)
    p.wait("v", sc)
    for pr in range(2):
        p.step("v", lambda E, o=mrow[0:2, 2 + pr:3 + pr], i_=mrow[0:2, 2 + pr:3 + pr],
               k=-SM_SCALE: E.tensor_scalar_mul(o, i_, k), ev)
    p.wait("t", ev)
    for pr in range(2):
        for hb in range(2):
            p.mmacc(psx[0:128, 0:1], [(bc2[0:2, hb * 128:(hb + 1) * 128], mrow[0:2, 2 + pr:3 + pr])], mm)
            p.wait("v", mm)
            p.step("v", lambda E, o=mbias[:, pr * 2 + hb:pr * 2 + hb + 1], i_=psx[0:128, 0:1]:
                   E.tensor_copy(o, i_), ev)
            p.wait("t", ev)
    p.barrier("t", "v", "s", "g", "y")

    cmask = consts.get("cmask4")
    for pr in range(2):
        for qc in range(NQ):
            kt_hi = (4 * qc + 4) if causal else NKT
            first = True
            for kt in range(kt_hi):
                sb_a = (ps1, ps3)[kt % 2]
                sb_b = (ps2, ps4)[kt % 2]
                p.wait("t", sc, sc.n - (2 if kt >= 2 else sc.n))
                p.step("t", lambda E, o=sb_a[:, :], l=qkv[("k", pr)][0:64, kt * 128:(kt + 1) * 128],
                       r=qkv[("q", pr)][0:64, qc * 512:(qc + 1) * 512]:
                       E.matmul(o, l, r, start=True, stop=True), mm)
                p.step("t", lambda E, o=sb_b[:, :], l=qkv[("k", pr)][64:128, kt * 128:(kt + 1) * 128],
                       r=qkv[("q", pr)][64:128, qc * 512:(qc + 1) * 512],
                       tp=(64, 0): E.matmul(o, l, r, start=True, stop=True, tile_position=tp), mm)
                slot_a = kt % 4
                slot_b = 4 + kt % 4
                p.wait("s", mm)
                if kt >= 4:
                    p.wait("s", mm, mm.n)
                p.step("s", lambda E, o=ptile[:, slot_a * 512:(slot_a + 1) * 512], i_=sb_a[:, :],
                       b=mbias[:, pr * 2:pr * 2 + 1], k=SM_SCALE:
                       E.activation(o, i_, AF.Exp, bias=b, scale=k), sc)
                p.step("s", lambda E, o=ptile[:, slot_b * 512:(slot_b + 1) * 512], i_=sb_b[:, :],
                       b=mbias[:, pr * 2 + 1:pr * 2 + 2], k=SM_SCALE:
                       E.activation(o, i_, AF.Exp, bias=b, scale=k), sc)
                diag = causal and kt >= 4 * qc
                if diag:
                    off = kt - 4 * qc
                    p.wait("v", sc)
                    for sl in (slot_a, slot_b):
                        p.step("v", lambda E, o=ptile[:, sl * 512:(sl + 1) * 512],
                               a=ptile[:, sl * 512:(sl + 1) * 512],
                               b=cmask[0:128, off * 512:(off + 1) * 512]: E.tensor_mul(o, a, b), ev)
                    p.wait("s", ev)
                p.wait("t", ev if diag else sc)
                h_a, h_b = pr * 2, pr * 2 + 1
                p.step("t", lambda E, o=psc1[0:65, :], l=vtm[h_a][:, kt * 66:kt * 66 + 65],
                       r=ptile[:, slot_a * 512:(slot_a + 1) * 512],
                       st_=first: E.matmul(o, l, r, start=st_, stop=True), mm)
                p.step("t", lambda E, o=psc2[0:65, :], l=vtm[h_b][:, kt * 66:kt * 66 + 65],
                       r=ptile[:, slot_b * 512:(slot_b + 1) * 512],
                       st_=first: E.matmul(o, l, r, start=st_, stop=True), mm)
                first = False
            p.wait("s", mm)
            p.step("s", lambda E, o=ctx_sb[0:65, 0:512], i_=psc1[0:65, :]: E.copy(o, i_), sc)
            p.step("s", lambda E, o=ctx_sb[0:65, 512:1024], i_=psc2[0:65, :]: E.copy(o, i_), sc)
            p.wait("v", sc)
            for hb in range(2):
                p.step("v", lambda E, o=ctx_sb[64:65, hb * 512:hb * 512 + 512],
                       i_=ctx_sb[64:65, hb * 512:hb * 512 + 512]: E.reciprocal(o, i_), ev)
            p.wait("t", ev)
            for hb in range(2):
                p.mmacc(psx[0:64, :], [(onesf[64:65, 0:64], ctx_sb[64:65, hb * 512:hb * 512 + 512])], mm)
                p.wait("v", mm)
                p.step("v", lambda E, o=ctxo[0:64, hb * 512:hb * 512 + 512],
                       a=ctx_sb[0:64, hb * 512:hb * 512 + 512], b=psx[0:64, :]:
                       E.tensor_mul(o, a, b), ev)
                p.wait("t", ev)
            p.wait("y", ev)
            for hb in range(2):
                h = pr * 2 + hb
                p.dma("y", o_ctx[h * 64:(h + 1) * 64, qc * 512:(qc + 1) * 512],
                      ctxo[0:64, hb * 512:hb * 512 + 512], st)
            p.wait("v", st, st.n)
            p.wait("t", sc)
    p.wait("y", st)
    return p.finish()


def _deinterleave_qkv(w_qkv, heads, with_bias=None, rope_perm=False):
    D_ = w_qkv.shape[0]
    out = {}
    for si, nm in enumerate(("q", "k", "v")):
        cols = []
        for h in heads:
            idx = [h * 192 + d * 3 + si for d in range(64)]
            if rope_perm and nm in ("q", "k"):
                idx = idx[0::2] + idx[1::2]
            cols.append(np.asarray(w_qkv[:, idx]))
        out[nm] = np.ascontiguousarray(np.concatenate(cols, 1))
    return out


def run_A4(inputs, a3_results):
    consts = const_inputs(["ident_b", "ones_f", "hsel2", "bc2"])
    w_qkv = np.asarray(inputs["L1_attn_w_qkv"])
    zers = np.zeros((256, 1), np.float32)
    in_maps = []
    for c in range(NCORE):
        b, j = c // 4, c % 4
        heads = list(range(4 * j, 4 * j + 4))
        ws = _deinterleave_qkv(w_qkv, heads)
        znf = np.stack([np.asarray(a3_results[4 * b + r]["znf"]) for r in range(4)])
        m = {"znf": znf, "qw": ws["q"], "kw": ws["k"], "vw": ws["v"],
             "qb": zers, "kb": zers, "vb": zers}
        m.update(consts)
        in_maps.append(m)
    nc = build_A4(rope=False, causal=False, SM_SCALE=0.125)
    res = run_bass_kernel_spmd(nc, in_maps, core_ids=list(range(NCORE)))
    return res.results



def build_A5():
    p = Prog()
    x2 = p.inp("x2", [CT, D])
    z0 = p.inp("z0", [CT, D])
    ctxs = p.inp("ctxs", [D, CT], BF16)
    w_proj = p.inp("w_proj", [D, D])
    b_proj = p.inp("b_proj", [1, D])
    gamma1 = p.inp("gamma1", [1, D])
    consts, load_consts = use_consts(p, ["ident_b", "ones_f"])
    o_x3 = p.outp("x3", [CT, D])
    o_xnf_tm = p.outp("xnf_tm", [CT, D])
    o_xnf = p.outp("xnf", [D, CT], BF16)

    wp = p.sbuf("wp", [128, 8 * D], BF16)
    cstr = p.sbuf("cstr", [128, 2 * 8 * 128], BF16)
    xstr = p.sbuf("xstr", [128, 2 * D], F32)
    zstr = p.sbuf("zstr", [128, 2 * D], F32)
    bpr = p.sbuf("bpr", [1, D], F32)
    g1r = p.sbuf("g1r", [1, D], F32)
    g1b = p.sbuf("g1b", [128, D], F32)
    ao = p.sbuf("ao", [128, D], F32)
    ln3 = p.sbuf("ln3", [128, D], F32)
    xnr = p.sbuf("xnr", [128, D], F32)
    xnb = p.sbuf("xnb", [128, D], BF16)
    xns = p.sbuf("xns", [128, D], BF16)
    scratch8 = p.sbuf("scratch8", [128, 2048], F32)
    stats = p.sbuf("stats", [128, 8], F32)
    fcon = p.sbuf("fcon", [128, 2], F32)

    ps1 = p.psum("ps1", [128, 512])
    ps2 = p.psum("ps2", [128, 512])
    psbT = p.psum("psbT", [128, 1024], BF16)

    ld = p.sem("ld")
    wv = p.sem("wv")
    mm = p.sem("mm")
    ev = p.sem("ev")
    sc = p.sem("sc")
    st = p.sem("st")

    idb = consts["ident_b"][:, :]
    onesf = consts["ones_f"]

    load_consts(ld)
    for k in range(8):
        p.dma("g", wp[:, k * D:(k + 1) * D], w_proj[k * 128:(k + 1) * 128, :], wv)
    p.dma("y", bpr[0:1, :], b_proj[:, :], ld)
    p.dma("y", g1r[0:1, :], gamma1[:, :], ld)
    p.op("g", lambda E: E.memset(fcon[:, 0:1], EPS))
    p.op("g", lambda E: E.memset(fcon[:, 1:2], 1.0))
    p.barrier("y", "g", "s", "v", "t")
    p.settle(ld, wv)
    for h_ in range(2):
        p.mmacc(ps1[:, :], [(onesf[0:1, :], g1r[0:1, h_ * 512:(h_ + 1) * 512])], mm)
        p.wait("v", mm)
        p.step("v", lambda E, o=g1b[:, h_ * 512:(h_ + 1) * 512], i_=ps1[:, :]: E.tensor_copy(o, i_), ev)
        p.wait("t", ev)

    for i in range(NTILE):
        buf = i % 2
        p.dma("y", xstr[:, buf * D:(buf + 1) * D], x2[i * 128:(i + 1) * 128, :], ld)
        p.dma("y", zstr[:, buf * D:(buf + 1) * D], z0[i * 128:(i + 1) * 128, :], ld)
        for k in range(8):
            p.dma("y", cstr[:, buf * 1024 + k * 128:buf * 1024 + (k + 1) * 128],
                  ctxs[k * 128:(k + 1) * 128, i * 128:(i + 1) * 128], ld)
        p.wait("t", ld)
        for nn in range(2):
            pb = (ps1, ps2)[nn]
            p.mmacc(pb[:, :],
                    [(cstr[:, buf * 1024 + k * 128:buf * 1024 + (k + 1) * 128],
                      wp[:, k * D + nn * 512:k * D + (nn + 1) * 512]) for k in range(8)]
                    + [(onesf[0:1, :], bpr[0:1, nn * 512:(nn + 1) * 512])], mm)
        p.wait("v", mm)
        p.wait("v", ld)
        for nn in range(2):
            pb = (ps1, ps2)[nn]
            sl = slice(nn * 512, (nn + 1) * 512)
            p.step("v", lambda E, o=ao[:, sl], a=pb[:, :], b=zstr[:, buf * D + nn * 512:buf * D + (nn + 1) * 512]:
                   E.tensor_add(o, a, b), ev)
        p.wait("v", ev)
        p.step("v", lambda E, o=xstr[:, buf * D:(buf + 1) * D], a=xstr[:, buf * D:(buf + 1) * D],
               b=ao[:, :]: E.tensor_add(o, a, b), ev)
        p.wait("v", ev)
        p.wait("t", ev)
        p.wait("y", ev)
        p.dma("y", o_x3[i * 128:(i + 1) * 128, :], xstr[:, buf * D:(buf + 1) * D], st)
        p.wait("s", ev)
        emit_ln(p, xstr[:, buf * D:(buf + 1) * D], ln3[:, :], scratch8, stats, sc, ev, epscol=fcon)
        p.wait("s", ev)
        emit_rms(p, ln3[:, :], xnr[:, :], scratch8, stats, sc, ev, D, fcon)
        p.wait("v", ev)
        p.step("v", lambda E, o=xnr[:, :], a=xnr[:, :], b=g1b[:, :]: E.tensor_mul(o, a, b), ev)
        p.wait("v", ev)
        p.step("v", lambda E, o=xnb[:, :], i_=xnr[:, :]: E.tensor_copy(o, i_), ev)
        p.wait("y", ev)
        p.dma("y", o_xnf_tm[i * 128:(i + 1) * 128, :], xnr[:, :], st)
        p.wait("t", ev)
        for k in range(8):
            p.step("t", lambda E, i_=xnb[:, k * 128:(k + 1) * 128], o=psbT[:, 0:128], idn=idb:
                   E.transpose(o, i_, idn), mm)
            p.wait("v", mm)
            p.step("v", lambda E, o=xns[:, k * 128:(k + 1) * 128], i_=psbT[:, 0:128]:
                   E.tensor_copy(o, i_), ev)
            p.wait("t", ev)
        p.wait("y", ev)
        for k in range(8):
            p.dma("y", o_xnf[k * 128:(k + 1) * 128, i * 128:(i + 1) * 128],
                  xns[:, k * 128:(k + 1) * 128], st)
        p.wait("v", st, st.n)
    p.wait("y", st)
    return p.finish()


def run_A5(inputs, a3_results, a4_results):
    consts = const_inputs(["ident_b", "ones_f"])
    in_maps = []
    for c in range(NCORE):
        b, j = c // 4, c % 4
        ctx_full = np.concatenate([np.asarray(a4_results[4 * b + r]["ctx"]) for r in range(4)], 0)
        m = {
            "x2": np.asarray(a3_results[c]["x2"]),
            "z0": np.asarray(a3_results[c]["z0"]),
            "ctxs": np.ascontiguousarray(ctx_full[:, j * CT:(j + 1) * CT]),
            "w_proj": np.asarray(inputs["L1_attn_w_proj"]),
            "b_proj": np.asarray(inputs["L1_attn_b_proj"]).reshape(1, -1),
            "gamma1": np.asarray(inputs["L1_ffn_gamma1"]).reshape(1, -1),
        }
        m.update(consts)
        in_maps.append(m)
    nc = build_A5()
    res = run_bass_kernel_spmd(nc, in_maps, core_ids=list(range(NCORE)))
    return res.results


def run_A6(inputs, a5_results):
    consts = const_inputs(["ident_b", "ones_f", "hsel2", "bc2", "cmask4", "swap64", "signcol"])
    qw = np.asarray(inputs["L1_ffn_q_w"])
    kw = np.asarray(inputs["L1_ffn_k_w"])
    vw = np.asarray(inputs["L1_ffn_v_w"])
    qb = np.asarray(inputs["L1_ffn_q_b"])
    kb = np.asarray(inputs["L1_ffn_k_b"])
    vb = np.asarray(inputs["L1_ffn_v_b"])
    theta = np.ascontiguousarray(np.tile(np.asarray(inputs["L1_ffn_theta"]).reshape(-1, 1), (4, 1)))
    trow = np.arange(S, dtype=np.float32).reshape(1, -1)
    in_maps = []
    for c in range(NCORE):
        b, j = c // 4, c % 4
        heads = list(range(4 * j, 4 * j + 4))
        perm = lambda h: [h * 64 + d for d in range(0, 64, 2)] + [h * 64 + d for d in range(1, 64, 2)]
        qcols = sum((perm(h) for h in heads), [])
        vcols = sum(([h * 64 + d for d in range(64)] for h in heads), [])
        znf = np.stack([np.asarray(a5_results[4 * b + r]["xnf"]) for r in range(4)])
        m = {"znf": znf,
             "qw": np.ascontiguousarray(qw[:, qcols]),
             "kw": np.ascontiguousarray(kw[:, qcols]),
             "vw": np.ascontiguousarray(vw[:, vcols]),
             "qb": np.ascontiguousarray(qb[qcols]).reshape(-1, 1),
             "kb": np.ascontiguousarray(kb[qcols]).reshape(-1, 1),
             "vb": np.ascontiguousarray(vb[vcols]).reshape(-1, 1),
             "theta": theta, "trow": trow}
        m.update(consts)
        in_maps.append(m)
    nc = build_A4(rope=True, causal=True, SM_SCALE=0.125)
    res = run_bass_kernel_spmd(nc, in_maps, core_ids=list(range(NCORE)))
    return res.results



def build_A7():
    p = Prog()
    ER, KACT, INTER = 31, 3, 1024
    x3 = p.inp("x3", [CT, D])
    xnf_tm = p.inp("xnf_tm", [CT, D])
    ctx2 = p.inp("ctx2", [D, CT], BF16)
    o_w = p.inp("o_w", [D, D])
    o_b = p.inp("o_b", [1, D])
    gamma2 = p.inp("gamma2", [1, D])
    sh_w1 = p.inp("sh_w1", [D, INTER])
    sh_b1 = p.inp("sh_b1", [128, INTER // 128])
    sh_w2 = p.inp("sh_w2", [INTER, D])
    sh_b2 = p.inp("sh_b2", [1, D])
    rw = p.inp("rw", [D, ER])
    rb = p.inp("rb", [1, ER])
    re_w1 = p.inp("re_w1", [ER, D, INTER])
    re_b1t = p.inp("re_b1t", [INTER, ER])
    re_w2 = p.inp("re_w2", [ER, INTER, D])
    re_b2 = p.inp("re_b2", [ER, D])
    consts, load_consts = use_consts(p, ["ident_b", "ident_f", "ones_f"])
    o_x4 = p.outp("x4", [CT, D])

    acc = p.sbuf("acc", [128, NTILE * D], F32)
    anf = p.sbuf("anf", [128, 8 * CT], BF16)
    innf = p.sbuf("innf", [128, 8 * CT], BF16)
    wstr = p.sbuf("wstr", [128, 8 * 128], BF16)
    w2s = p.sbuf("w2s", [128, 2 * D], BF16)
    wo = p.sbuf("wo", [128, 8 * D], BF16)
    cstr = p.sbuf("cstr", [128, 2 * 8 * 128], BF16)
    xstr = p.sbuf("xstr", [128, 2 * D], F32)
    obr = p.sbuf("obr", [1, D], F32)
    g2r = p.sbuf("g2r", [1, D], F32)
    sb2r = p.sbuf("sb2r", [1, D], F32)
    rbr = p.sbuf("rbr", [1, ER], F32)
    g2b = p.sbuf("g2b", [128, D], F32)
    b1c = p.sbuf("b1c", [128, 8], F32)
    r1bt = p.sbuf("r1bt", [128, 8 * ER], F32)
    r2bs = p.sbuf("r2bs", [ER, D], F32)
    b2row = p.sbuf("b2row", [1, D], F32)
    an_t = p.sbuf("an_t", [128, D], F32)
    an_b = p.sbuf("an_b", [128, D], BF16)
    gts = p.sbuf("gts", [128, NTILE * 32], F32)
    rl = p.sbuf("rl", [128, 32], F32)
    rp = p.sbuf("rp", [128, 32], F32)
    rmsk = p.sbuf("rmsk", [128, 32], F32)
    rstat = p.sbuf("rstat", [128, 8], F32)
    dtmp = p.sbuf("dtmp", [128, 512], F32)
    scratch8 = p.sbuf("scratch8", [128, 2048], F32)
    stats = p.sbuf("stats", [128, 8], F32)
    fcon = p.sbuf("fcon", [128, 2], F32)

    ps1 = p.psum("ps1", [128, 512])
    ps2 = p.psum("ps2", [128, 512])
    ps3 = p.psum("ps3", [128, 512])
    psd = p.psum("psd", [128, 2048])
    psbT = p.psum("psbT", [128, 1024], BF16)

    ld = p.sem("ld")
    wv = p.sem("wv")
    mm = p.sem("mm")
    ev = p.sem("ev")
    sc = p.sem("sc")
    st = p.sem("st")

    idb = consts["ident_b"][:, :]
    idf = consts["ident_f"][:, :]
    onesf = consts["ones_f"]

    load_consts(ld)
    for k in range(8):
        p.dma("g", wo[:, k * D:(k + 1) * D], o_w[k * 128:(k + 1) * 128, :], wv)
    p.dma("y", obr[0:1, :], o_b[:, :], ld)
    p.dma("y", g2r[0:1, :], gamma2[:, :], ld)
    p.dma("y", sb2r[0:1, :], sh_b2[:, :], ld)
    p.dma("y", rbr[0:1, :], rb[:, :], ld)
    p.dma("y", b1c[:, :], sh_b1[:, :], ld)
    for k in range(8):
        p.dma("y", r1bt[:, k * ER:(k + 1) * ER], re_b1t[k * 128:(k + 1) * 128, :], ld)
    p.dma("y", r2bs[0:ER, :], re_b2[:, :], ld)
    p.op("g", lambda E: E.memset(fcon[:, 0:1], EPS))
    p.op("g", lambda E: E.memset(fcon[:, 1:2], 1.0))
    p.op("v", lambda E: E.memset(gts[:, :], 0.0))
    p.barrier("y", "g", "s", "v", "t")
    p.settle(ld, wv)
    for h_ in range(2):
        p.mmacc(ps1[:, :], [(onesf[0:1, :], g2r[0:1, h_ * 512:(h_ + 1) * 512])], mm)
        p.wait("v", mm)
        p.step("v", lambda E, o=g2b[:, h_ * 512:(h_ + 1) * 512], i_=ps1[:, :]: E.tensor_copy(o, i_), ev)
        p.wait("t", ev)

    rwb = p.sbuf("rwb", [128, 8 * 32], BF16)
    for k in range(8):
        p.dma("g", rwb[:, k * 32:k * 32 + ER], rw[k * 128:(k + 1) * 128, :], wv)
    for i in range(NTILE):
        buf = i % 2
        p.dma("y", xstr[:, buf * D:(buf + 1) * D], xnf_tm[i * 128:(i + 1) * 128, :], ld)
        for k in range(8):
            p.dma("y", cstr[:, buf * 1024 + k * 128:buf * 1024 + (k + 1) * 128],
                  ctx2[k * 128:(k + 1) * 128, i * 128:(i + 1) * 128], ld)
        p.wait("t", ld)
        p.wait("t", wv)
        for nn in range(2):
            pb = (ps1, ps2)[nn]
            p.mmacc(pb[:, :],
                    [(cstr[:, buf * 1024 + k * 128:buf * 1024 + (k + 1) * 128],
                      wo[:, k * D + nn * 512:k * D + (nn + 1) * 512]) for k in range(8)]
                    + [(onesf[0:1, :], obr[0:1, nn * 512:(nn + 1) * 512])], mm)
        p.wait("v", mm)
        p.wait("v", ld)
        for nn in range(2):
            pb = (ps1, ps2)[nn]
            p.step("v", lambda E, o=acc[:, i * D + nn * 512:i * D + (nn + 1) * 512], a=pb[:, :],
                   b=xstr[:, buf * D + nn * 512:buf * D + (nn + 1) * 512]: E.tensor_add(o, a, b), ev)
        p.wait("t", ev)
        p.wait("s", ev)
        emit_rms(p, acc[:, i * D:(i + 1) * D], an_t[:, :], scratch8, stats, sc, ev, D, fcon)
        p.wait("v", ev)
        p.step("v", lambda E, o=an_t[:, :], a=an_t[:, :], b=g2b[:, :]: E.tensor_mul(o, a, b), ev)
        p.wait("v", ev)
        p.step("v", lambda E, o=an_b[:, :], i_=an_t[:, :]: E.tensor_copy(o, i_), ev)
        p.step("v", lambda E, o=acc[:, i * D:(i + 1) * D], a=acc[:, i * D:(i + 1) * D],
               b=an_t[:, :]: E.tensor_add(o, a, b), ev)
        p.wait("t", ev)
        for k in range(8):
            p.step("t", lambda E, i_=an_b[:, k * 128:(k + 1) * 128], o=psbT[:, 0:128], idn=idb:
                   E.transpose(o, i_, idn), mm)
            p.wait("v", mm)
            p.step("v", lambda E, o=anf[:, k * CT + i * 128:k * CT + (i + 1) * 128],
                   i_=psbT[:, 0:128]: E.tensor_copy(o, i_), ev)
            p.wait("t", ev)
        p.mmacc(ps3[:, 0:ER],
                [(anf[:, k * CT + i * 128:k * CT + (i + 1) * 128], rwb[:, k * 32:k * 32 + ER])
                 for k in range(8)] + [(onesf[0:1, :], rbr[0:1, :])], mm)
        p.wait("v", mm)
        p.step("v", lambda E, o=rl[:, 0:ER], i_=ps3[:, 0:ER]: E.tensor_copy(o, i_), ev)
        p.wait("t", ev)
        p.wait("v", ev)
        p.step("v", lambda E, o=rstat[:, 0:1], i_=rl[:, 0:ER]: E.reduce_max(o, i_, axis=AX.X), ev)
        p.wait("v", ev)
        p.step("v", lambda E, o=rstat[:, 1:2], i_=rstat[:, 0:1]: E.tensor_scalar_mul(o, i_, -1.0), ev)
        p.wait("s", ev)
        p.step("s", lambda E, o=rp[:, 0:ER], i_=rl[:, 0:ER], b=rstat[:, 1:2], acc_=rstat[:, 2:3]:
               E.activation(o, i_, AF.Exp, bias=b, accum_out=acc_), sc)
        p.wait("v", sc)
        p.step("v", lambda E, o=rstat[:, 3:4], i_=rstat[:, 2:3]: E.reciprocal(o, i_), ev)
        p.wait("v", ev)
        p.step("v", lambda E, o=rp[:, 0:ER], i_=rp[:, 0:ER], s_=rstat[:, 3:4]:
               E.tensor_scalar_mul(o, i_, s_), ev)
        p.wait("v", ev)
        p.step("v", lambda E, o=rmsk[:, 0:ER], i_=rp[:, 0:ER]: E.tensor_copy(o, i_), ev)
        for kk in range(KACT):
            p.wait("v", ev)
            p.step("v", lambda E, o=rstat[:, 4:5], i_=rmsk[:, 0:ER]: E.reduce_max(o, i_, axis=AX.X), ev)
            p.wait("v", ev)
            p.step("v", lambda E, o=dtmp[:, 0:ER], i_=rmsk[:, 0:ER], s_=rstat[:, 4:5]:
                   E.tensor_scalar(o, i_, s_, None, ALU.is_ge), ev)
            p.wait("v", ev)
            p.step("v", lambda E, o=gts[:, i * 32:i * 32 + ER], i0=dtmp[:, 0:ER],
                   i1=gts[:, i * 32:i * 32 + ER], s_=rstat[:, 4:5]:
                   E.scalar_tensor_tensor(o, i0, s_, i1, ALU.mult, ALU.add), ev)
            p.step("v", lambda E, o=rmsk[:, 0:ER], i0=dtmp[:, 0:ER], i1=rmsk[:, 0:ER]:
                   E.scalar_tensor_tensor(o, i0, -1e30, i1, ALU.mult, ALU.add), ev)
        p.wait("y", ev)
        p.wait("v", st, st.n)
    p.barrier("t", "v", "s", "y", "g")

    for m in range(8):
        for k in range(8):
            p.dma("g", wstr[:, k * 128:(k + 1) * 128],
                  sh_w1[k * 128:(k + 1) * 128, m * 128:(m + 1) * 128], wv)
        p.wait("t", wv)
        for n in range(4):
            pb = (ps1, ps2)[n % 2]
            p.mmacc(pb[:, :],
                    [(wstr[:, k * 128:(k + 1) * 128], anf[:, k * CT + n * 512:k * CT + (n + 1) * 512])
                     for k in range(8)], mm)
            p.wait("s", mm)
            p.step("s", lambda E, o=innf[:, m * CT + n * 512:m * CT + (n + 1) * 512],
                   i_=pb[:, :], b=b1c[:, m:m + 1]: E.activation(o, i_, AF.Gelu, bias=b), sc)
            p.wait("t", sc, sc.n - 1)
        p.wait("g", sc, sc.n)
    p.barrier("t", "s", "g")
    for i in range(NTILE):
        for k in range(8):
            p.dma("g", w2s[:, (k % 2) * D:(k % 2) * D + D],
                  sh_w2[k * 128:(k + 1) * 128, :], wv)
            p.wait("t", wv)
            for nn in range(2):
                fin = (nn == 1)
                p.step("t", lambda E, o=psd[:, nn * 512:(nn + 1) * 512],
                       l=innf[:, k * CT + i * 128:k * CT + (i + 1) * 128],
                       r=w2s[:, (k % 2) * D + nn * 512:(k % 2) * D + (nn + 1) * 512],
                       st_=(k == 0): E.matmul(o, l, r, start=st_, stop=True), mm if fin else None)
            p.wait("g", mm, mm.n)
        p.step("t", lambda E, o=psd[:, 0:512], l=onesf[0:1, :], r=sb2r[0:1, 0:512]:
               E.matmul(o, l, r, start=False, stop=True), mm)
        p.step("t", lambda E, o=psd[:, 512:1024], l=onesf[0:1, :], r=sb2r[0:1, 512:1024]:
               E.matmul(o, l, r, start=False, stop=True), mm)
        p.wait("v", mm)
        for nn in range(2):
            p.step("v", lambda E, o=acc[:, i * D + nn * 512:i * D + (nn + 1) * 512],
                   a=acc[:, i * D + nn * 512:i * D + (nn + 1) * 512],
                   b=psd[:, nn * 512:(nn + 1) * 512]: E.tensor_add(o, a, b), ev)
            p.wait("v", ev)
        p.wait("t", ev)
    p.barrier("t", "v", "s", "g", "y")

    for e in range(31):
        p.mmacc(ps3[0:1, 0:512], [(idf[0:ER, e:e + 1], r2bs[0:ER, 0:512])], mm)
        p.wait("v", mm)
        p.step("v", lambda E, o=b2row[0:1, 0:512], i_=ps3[0:1, 0:512]: E.tensor_copy(o, i_), ev)
        p.wait("t", ev)
        p.mmacc(ps3[0:1, 0:512], [(idf[0:ER, e:e + 1], r2bs[0:ER, 512:1024])], mm)
        p.wait("v", mm)
        p.step("v", lambda E, o=b2row[0:1, 512:1024], i_=ps3[0:1, 0:512]: E.tensor_copy(o, i_), ev)
        p.wait("t", ev)
        for m in range(8):
            for k in range(8):
                p.dma("g", wstr[:, k * 128:(k + 1) * 128],
                      re_w1[e, k * 128:(k + 1) * 128, m * 128:(m + 1) * 128], wv)
            p.wait("t", wv)
            for n in range(4):
                pb = (ps1, ps2)[n % 2]
                p.mmacc(pb[:, :],
                        [(wstr[:, k * 128:(k + 1) * 128], anf[:, k * CT + n * 512:k * CT + (n + 1) * 512])
                         for k in range(8)], mm)
                p.wait("s", mm)
                p.step("s", lambda E, o=innf[:, m * CT + n * 512:m * CT + (n + 1) * 512],
                       i_=pb[:, :], b=r1bt[:, m * ER + e:m * ER + e + 1]:
                       E.activation(o, i_, AF.Gelu, bias=b), sc)
                p.wait("t", sc, sc.n - 1)
            p.wait("g", sc, sc.n)
        p.barrier("t", "s", "g")
        for i in range(NTILE):
            for k in range(8):
                p.dma("g", w2s[:, (k % 2) * D:(k % 2) * D + D],
                      re_w2[e, k * 128:(k + 1) * 128, :], wv)
                p.wait("t", wv)
                for nn in range(2):
                    fin = (nn == 1)
                    p.step("t", lambda E, o=psd[:, nn * 512:(nn + 1) * 512],
                           l=innf[:, k * CT + i * 128:k * CT + (i + 1) * 128],
                           r=w2s[:, (k % 2) * D + nn * 512:(k % 2) * D + (nn + 1) * 512],
                           st_=(k == 0): E.matmul(o, l, r, start=st_, stop=True), mm if fin else None)
                p.wait("g", mm, mm.n)
            p.step("t", lambda E, o=psd[:, 0:512], l=onesf[0:1, :], r=b2row[0:1, 0:512]:
                   E.matmul(o, l, r, start=False, stop=True), mm)
            p.step("t", lambda E, o=psd[:, 512:1024], l=onesf[0:1, :], r=b2row[0:1, 512:1024]:
                   E.matmul(o, l, r, start=False, stop=True), mm)
            p.wait("v", mm)
            for nn in range(2):
                p.step("v", lambda E, o=dtmp[:, :], i_=psd[:, nn * 512:(nn + 1) * 512],
                       s_=gts[:, i * 32 + e:i * 32 + e + 1]: E.tensor_scalar_mul(o, i_, s_), ev)
                p.wait("v", ev)
                p.step("v", lambda E, o=acc[:, i * D + nn * 512:i * D + (nn + 1) * 512],
                       a=acc[:, i * D + nn * 512:i * D + (nn + 1) * 512], b=dtmp[:, :]:
                       E.tensor_add(o, a, b), ev)
                p.wait("v", ev)
            p.wait("t", ev)
    p.barrier("t", "v", "s", "g", "y")

    for i in range(NTILE):
        buf = i % 2
        p.dma("y", xstr[:, buf * D:(buf + 1) * D], x3[i * 128:(i + 1) * 128, :], ld)
        p.wait("v", ld)
        p.step("v", lambda E, o=xstr[:, buf * D:(buf + 1) * D], a=xstr[:, buf * D:(buf + 1) * D],
               b=acc[:, i * D:(i + 1) * D]: E.tensor_add(o, a, b), ev)
        p.wait("y", ev)
        p.dma("y", o_x4[i * 128:(i + 1) * 128, :], xstr[:, buf * D:(buf + 1) * D], st)
        p.wait("v", st, st.n - 16)
    p.wait("y", st)
    return p.finish()


def run_A7(inputs, a5_results, a6_results):
    consts = const_inputs(["ident_b", "ident_f", "ones_f"])
    in_maps = []
    for c in range(NCORE):
        b, j = c // 4, c % 4
        ctx_full = np.concatenate([np.asarray(a6_results[4 * b + r]["ctx"]) for r in range(4)], 0)
        m = {
            "x3": np.asarray(a5_results[c]["x3"]),
            "xnf_tm": np.asarray(a5_results[c]["xnf_tm"]),
            "ctx2": np.ascontiguousarray(ctx_full[:, j * CT:(j + 1) * CT]),
            "o_w": np.asarray(inputs["L1_ffn_o_w"]),
            "o_b": np.asarray(inputs["L1_ffn_o_b"]).reshape(1, -1),
            "gamma2": np.asarray(inputs["L1_ffn_gamma2"]).reshape(1, -1),
            "sh_w1": np.asarray(inputs["L1_ffn_sh_w1"]),
            "sh_b1": np.ascontiguousarray(np.asarray(inputs["L1_ffn_sh_b1"]).reshape(8, 128).T),
            "sh_w2": np.asarray(inputs["L1_ffn_sh_w2"]),
            "sh_b2": np.asarray(inputs["L1_ffn_sh_b2"]).reshape(1, -1),
            "rw": np.asarray(inputs["L1_ffn_router_w"]),
            "rb": np.asarray(inputs["L1_ffn_router_b"]).reshape(1, -1),
            "re_w1": np.asarray(inputs["L1_ffn_re_w1"]),
            "re_b1t": np.ascontiguousarray(np.asarray(inputs["L1_ffn_re_b1"]).T),
            "re_w2": np.asarray(inputs["L1_ffn_re_w2"]),
            "re_b2": np.asarray(inputs["L1_ffn_re_b2"]),
        }
        m.update(consts)
        in_maps.append(m)
    nc = build_A7()
    res = run_bass_kernel_spmd(nc, in_maps, core_ids=list(range(NCORE)))
    return res.results
